# version 03
# baseline (speedup 1.0000x reference)
import contextlib

import numpy as np

import concourse.bass as bass
import concourse.mybir as mybir
from concourse.bass_utils import run_bass_kernel_spmd

F32 = mybir.dt.float32
BF16 = mybir.dt.bfloat16
I8 = mybir.dt.int8
ALU = mybir.AluOpType
ACTF = mybir.ActivationFunctionType
AXX = mybir.AxisListType.X

L = 2048
D = 1024
E = 2048
N = 16
II = 4096
TG = 128
NG = L // TG
CH = 512
NCH = L // CH
EC = E // 128
DC = D // 128
EPS = 1e-5


class Ctr:
    def __init__(self, sem):
        self.sem = sem
        self.n = 0

    def inc(self, by=1):
        self.n += by
        return self.n


def build_program():
    nc = bass.Bass("TRN2", target_bir_lowering=False)

    x_in = nc.dram_tensor("x", [L, D], F32, kind="ExternalInput").ap()
    w_in = nc.dram_tensor("w_in", [2 * E, D], F32, kind="ExternalInput").ap()
    a_log = nc.dram_tensor("a_log", [E, N], F32, kind="ExternalInput").ap()
    w_b = nc.dram_tensor("w_b", [N, E], F32, kind="ExternalInput").ap()
    w_c = nc.dram_tensor("w_c", [N, E], F32, kind="ExternalInput").ap()
    w_dt = nc.dram_tensor("w_dt", [E, E], F32, kind="ExternalInput").ap()
    b_dt = nc.dram_tensor("b_dt", [E], F32, kind="ExternalInput").ap()
    d_vec = nc.dram_tensor("d_vec", [E], F32, kind="ExternalInput").ap()
    gate = nc.dram_tensor("gate", [1], F32, kind="ExternalInput").ap()
    w_out = nc.dram_tensor("w_out", [D, E], F32, kind="ExternalInput").ap()
    w_g = nc.dram_tensor("w_g", [II, D], F32, kind="ExternalInput").ap()
    w_u = nc.dram_tensor("w_u", [II, D], F32, kind="ExternalInput").ap()
    w_d = nc.dram_tensor("w_d", [D, II], F32, kind="ExternalInput").ap()
    out = nc.dram_tensor("out", [L, D], F32, kind="ExternalOutput").ap()

    win_s = nc.dram_tensor("win_s", [2 * E, D], BF16).ap()
    wdt_s = nc.dram_tensor("wdt_s", [E, E], BF16).ap()
    wo_s = nc.dram_tensor("wo_s", [D, E], BF16).ap()
    wb_s = nc.dram_tensor("wb_s", [N, E], BF16).ap()
    wc_s = nc.dram_tensor("wc_s", [N, E], BF16).ap()
    wg_s = nc.dram_tensor("wg_s", [II, D], BF16).ap()
    wu_s = nc.dram_tensor("wu_s", [II, D], BF16).ap()
    wd_s = nc.dram_tensor("wd_s", [D, II], BF16).ap()
    wgi_s = nc.dram_tensor("wgi_s", [II, D], I8).ap()
    wui_s = nc.dram_tensor("wui_s", [II, D], I8).ap()
    wdi_s = nc.dram_tensor("wdi_s", [D, II], I8).ap()
    iqi_s = nc.dram_tensor("iqi_s", [L, D], I8).ap()
    q2i_s = nc.dram_tensor("q2i_s", [L, II], I8).ap()
    xn_s = nc.dram_tensor("xn_s", [L, D], BF16).ap()
    iq_s = nc.dram_tensor("iq_s", [L, D], BF16).ap()
    q2_s = nc.dram_tensor("q2_s", [L, II], BF16).ap()
    x1_s = nc.dram_tensor("x1_s", [L, D], F32).ap()
    dt_s = nc.dram_tensor("dt_s", [EC, 128, L], F32).ap()
    sz_s = nc.dram_tensor("sz_s", [EC, 128, L], BF16).ap()
    yin_s = nc.dram_tensor("yin_s", [EC, 128, L], BF16).ap()
    bc_d = nc.dram_tensor("bc_d", [32, L], BF16).ap()

    stk = contextlib.ExitStack()
    T = lambda *a, **k: stk.enter_context(nc.sbuf_tensor(*a, **k))
    P = lambda *a, **k: stk.enter_context(nc.psum_tensor(*a, **k))
    SEM = lambda name: stk.enter_context(nc.semaphore(name))

    with stk:
        A_all = T("A_all", [128, EC * N], F32)
        D_all = T("D_all", [128, EC], F32)
        bdt_all = T("bdt_all", [128, EC], F32)
        cols = T("cols", [128, 16], F32)
        ones_f = T("ones_f", [1, 128], F32)
        onescol_f = T("onescol_f", [128, 1], F32)
        h_state = T("h_state", [128, EC * N], F32)
        stinv = T("stinv", [128, NG], F32)
        sgam = T("sgam", [128, NG], F32)
        suam = T("suam", [128, NG], F32)
        beta = T("beta", [128, NG], F32)
        tiny = T("tiny", [128, 8], F32)
        BC_sb = T("BC_sb", [32, L], BF16)
        xnF = T("xnF", [128, DC * L], BF16, is_transpose_dest=True)
        cst = T("cst", [128, 8], F32)

        big_ps = [P(f"big_ps{i}", [128, 512], F32) for i in range(4)]
        mid_ps = [P(f"mid_ps{i}", [128, 512], F32) for i in range(4)]

        PE = Ctr(SEM("pe"))
        SC = Ctr(SEM("sc"))
        DV = Ctr(SEM("dv"))
        GP = Ctr(SEM("gp"))
        WST = Ctr(SEM("wst"))
        sBC = Ctr(SEM("sbc"))
        sXF = Ctr(SEM("sxf"))
        sCON = Ctr(SEM("scon"))
        sBB = Ctr(SEM("sbb"))
        sYB = Ctr(SEM("syb"))
        sXI = Ctr(SEM("sxi"))
        sBCD = Ctr(SEM("sbcd"))
        OUT = Ctr(SEM("outsem"))
        PREW = [Ctr(SEM(f"sprew{i}")) for i in range(5)]

        def slot_sems(name, k):
            return [Ctr(SEM(f"{name}{i}")) for i in range(k)]

        sPW = slot_sems("spw", 2)
        sXA = slot_sems("sxa", 2)
        sDT = slot_sems("sdt", 3)
        sSZ = slot_sems("ssz", 3)
        sFX = slot_sems("sfx", 2)
        sQF = slot_sems("sqf", 4)
        sXI = slot_sems("sxi", 4)
        sXN = slot_sems("sxn", 2)
        sSZS = slot_sems("sszs", 2)
        sDTS = slot_sems("sdts", 4)
        sYIN = slot_sems("syin", 2)
        sX1S = slot_sems("sx1s", 2)
        sIQS = slot_sems("siqs", 2)
        sQ2S = slot_sems("sq2s", 2)
        sWQ = slot_sems("swq", 2)
        sOUTS = slot_sems("souts", 4)
        sGPP = slot_sems("sgpp", 2)
        sGPF = slot_sems("sgpf", 1)
        sGPH = slot_sems("sgph", 1)

        def sl_issue(dma, ss, i):
            c = ss[i % len(ss)]
            dma.then_inc(c.sem, 16)
            return (i % len(ss), c.inc(16))

        def sl_wait(eng, ss, mark):
            if mark:
                eng.wait_ge(ss[mark[0]].sem, mark[1])

        def sl_wait_all(eng, ss):
            for c in ss:
                if c.n:
                    eng.wait_ge(c.sem, c.n)

        _SEQ_ONLY = {"wait_ge", "nop", "sem_inc", "dma_start", "drain",
                     "maybe_wait_ge", "wait_op"}

        def _ap_name(x):
            try:
                return x.tensor.name
            except AttributeError:
                return None

        class _Trk:
            DEPTH = 6

            def __init__(self, eng):
                self._eng = eng
                self.last = None
                self._hist = []

            def __getattr__(self, name):
                f = getattr(self._eng, name)
                if callable(f) and not name.startswith("_"):
                    trk = self

                    def g(*a, _f=f, _name=name, **k):
                        if _name not in _SEQ_ONLY:
                            aps = [x for x in list(a) + list(k.values())
                                   if _ap_name(x) is not None]
                            wr = {_ap_name(aps[0])} if aps else set()
                            for kk in ("out", "accum_out"):
                                if _ap_name(k.get(kk)) is not None:
                                    wr.add(_ap_name(k[kk]))
                            rd = {_ap_name(x) for x in aps} - wr
                            hazard = any((rd & hwr) or (wr & hwr) or (wr & hrd)
                                         for hrd, hwr in trk._hist)
                            if hazard:
                                trk._eng.drain()
                                trk._hist = []
                        r = _f(*a, **k)
                        if _name not in _SEQ_ONLY and hasattr(r, "then_inc"):
                            trk.last = r
                            trk._hist.append((rd, wr))
                            if len(trk._hist) > trk.DEPTH:
                                trk._hist.pop(0)
                        return r
                    return g
                return f

        TEN = _Trk(nc.tensor)
        VEC = _Trk(nc.vector)
        SCA = _Trk(nc.scalar)
        GPS = _Trk(nc.gpsimd)
        SYN = nc.sync

        _marked = {}

        def _mk(trk, ctr, key):
            if trk.last is _marked.get(key):
                return ctr.n
            _marked[key] = trk.last
            trk.last.then_inc(ctr.sem, 1)
            return ctr.inc()

        def pe_mark():
            return _mk(TEN, PE, "pe")

        def sc_mark():
            return _mk(SCA, SC, "sc")

        def dv_mark():
            return _mk(VEC, DV, "dv")

        def gp_mark():
            return _mk(GPS, GP, "gp")

        with nc.Block():
            for i, v in enumerate((EPS, 1.0, 0.5, -0.5, -1.0)):
                VEC.memset(cst[:, i:i + 1], v)
                nc.const_aps.register(F32, v, cst[:, i:i + 1])
            VEC.memset(ones_f[:, :], 1.0)
            VEC.memset(onescol_f[:, :], 1.0)
            VEC.memset(h_state[:, :], 0.0)
            init_dv = dv_mark()
            SCA.wait_ge(DV.sem, init_dv)
            TEN.wait_ge(DV.sem, init_dv)

            for i, (dst, src) in enumerate(((win_s, w_in), (wdt_s, w_dt),
                                            (wo_s, w_out), (wb_s, w_b),
                                            (wc_s, w_c))):
                GPS.dma_start(out=dst[:, :], in_=src[:, :]
                              ).then_inc(PREW[i].sem, 16)
                PREW[i].inc(16)

            with nc.sbuf_tensor("at_l", [128, EC * N], F32) as at_l:
                for c in range(EC):
                    SYN.dma_start(out=at_l[:, c * N:(c + 1) * N],
                                  in_=a_log[c * 128:(c + 1) * 128, :]
                                  ).then_inc(sCON.sem, 16)
                    sCON.inc(16)
                    SYN.dma_start(out=bdt_all[:, c:c + 1],
                                  in_=b_dt[c * 128:(c + 1) * 128]
                                  ).then_inc(sCON.sem, 16)
                    sCON.inc(16)
                    SYN.dma_start(out=D_all[:, c:c + 1],
                                  in_=d_vec[c * 128:(c + 1) * 128]
                                  ).then_inc(sCON.sem, 16)
                    sCON.inc(16)
                SYN.dma_start(out=tiny[0:1, 0:1], in_=gate[:]
                              ).then_inc(sCON.sem, 16)
                sCON.inc(16)

                SCA.wait_ge(sCON.sem, sCON.n)
                SCA.activation(A_all[:, :], at_l[:, :], ACTF.Exp)
                SCA.activation(tiny[0:1, 1:2], tiny[0:1, 0:1], ACTF.Exp,
                               scale=-1.0)
                sc_prep = sc_mark()
                VEC.wait_ge(SC.sem, sc_prep)
                VEC.tensor_scalar_mul(A_all[:, :], A_all[:, :], -1.0)
                VEC.tensor_scalar_add(tiny[0:1, 2:3], tiny[0:1, 1:2], 1.0)
                VEC.reciprocal(tiny[0:1, 3:4], tiny[0:1, 2:3])
                dv_sig = dv_mark()
                TEN.wait_ge(DV.sem, dv_sig)
                TEN.matmul(mid_ps[0][:, 0:1], ones_f[0:1, :],
                           tiny[0:1, 3:4], start=True, stop=True)
                pe_g = pe_mark()
                VEC.wait_ge(PE.sem, pe_g)
                VEC.tensor_copy(cols[:, 0:1], mid_ps[0][:, 0:1])
                dv_mark()

            with (nc.sbuf_tensor("pwb0", [128, II], F32) as pw0,
                  nc.sbuf_tensor("pwb1", [128, II], F32) as pw1,
                  nc.sbuf_tensor("pq0", [128, II], BF16) as pq0,
                  nc.sbuf_tensor("pq1", [128, II], BF16) as pq1,
                  nc.sbuf_tensor("pt0", [128, II], F32) as pt0,
                  nc.sbuf_tensor("pt1", [128, II], F32) as pt1,
                  nc.sbuf_tensor("pacc", [128, 2], F32) as pacc,
                  nc.sbuf_tensor("pscr", [128, II], F32) as pscr,
                  nc.sbuf_tensor("pca", [128, II], F32) as pca,
                  nc.sbuf_tensor("pcb", [128, II], F32) as pcb):
                pwb = [pw0, pw1]
                pqb = [pq0, pq1]
                ptb = [pt0, pt1]
                pli = 0
                qsi = 0
                pl_gate = {}
                st_q = {}
                w_done = {}
                for wsrc, wdst, wint, sci, mci in (
                        (w_g, wg_s, wgi_s, 1, 4), (w_u, wu_s, wui_s, 2, 5),
                        (w_d, wd_s, wdi_s, 3, 6)):
                    cw = II if wsrc is w_d else D
                    nrow = (D if wsrc is w_d else II) // 128
                    dv_acc = {}
                    for r in range(nrow):
                        SYN.wait_ge(SC.sem, pl_gate.get(pli - 2, sc_prep))
                        m = sl_issue(SYN.dma_start(
                            out=pwb[pli % 2][:, 0:cw],
                            in_=wsrc[r * 128:(r + 1) * 128, :]), sPW, pli)
                        SCA.wait_ge(DV.sem, dv_acc.get(r - 1, 0))
                        sl_wait(SCA, sPW, m)
                        SCA.activation(pscr[:, 0:cw], pwb[pli % 2][:, 0:cw],
                                       ACTF.Abs, accum_out=tiny[:, 4:5])
                        pl_gate[pli] = sc_mark()
                        pli += 1
                        VEC.wait_ge(SC.sem, pl_gate[pli - 1])
                        if r == 0:
                            VEC.tensor_copy(pacc[:, 0:1], tiny[:, 4:5])
                        else:
                            VEC.tensor_add(pacc[:, 0:1], pacc[:, 0:1],
                                           tiny[:, 4:5])
                        dv_acc[r] = dv_mark()
                    TEN.wait_ge(DV.sem, dv_acc[nrow - 1])
                    TEN.matmul(mid_ps[0][0:1, 0:1], onescol_f[:, 0:1],
                               pacc[:, 0:1], start=True, stop=True)
                    pe_tot = pe_mark()
                    VEC.wait_ge(PE.sem, pe_tot)
                    VEC.tensor_scalar_mul(tiny[0:1, 5:6], mid_ps[0][0:1, 0:1],
                                          1.0 / (II * D))
                    VEC.tensor_scalar_max(tiny[0:1, 5:6], tiny[0:1, 5:6], 1e-5)
                    VEC.reciprocal(tiny[0:1, 6:7], tiny[0:1, 5:6])
                    dv_sc = dv_mark()
                    TEN.wait_ge(DV.sem, dv_sc)
                    TEN.matmul(mid_ps[0][:, 1:3], ones_f[0:1, :],
                               tiny[0:1, 5:7], start=True, stop=True)
                    pe_bc = pe_mark()
                    VEC.wait_ge(PE.sem, pe_bc)
                    VEC.tensor_copy(cols[:, mci:mci + 1], mid_ps[0][:, 1:2])
                    VEC.tensor_copy(cols[:, sci:sci + 1], mid_ps[0][:, 2:3])
                    dv_cols = dv_mark()
                    dv_q = {}
                    gp_c = {}
                    for r in range(nrow):
                        SYN.wait_ge(SC.sem, pl_gate.get(pli - 2, 0))
                        SYN.wait_ge(DV.sem, dv_cols)
                        m = sl_issue(SYN.dma_start(
                            out=pwb[pli % 2][:, 0:cw],
                            in_=wsrc[r * 128:(r + 1) * 128, :]), sPW, pli)
                        SCA.wait_ge(DV.sem, dv_q.get(r - 2, dv_cols))
                        sl_wait(SCA, sPW, m)
                        SCA.activation(ptb[r % 2][:, 0:cw],
                                       pwb[pli % 2][:, 0:cw], ACTF.Copy,
                                       scale=cols[:, sci:sci + 1])
                        pl_gate[pli] = sc_mark()
                        pli += 1
                        VEC.wait_ge(SC.sem, pl_gate[pli - 1])
                        sl_wait(VEC, sGPP, gp_c.get(r - 2))
                        VEC.tensor_scalar(ptb[r % 2][:, 0:cw],
                                          ptb[r % 2][:, 0:cw], 1.0, -1.0,
                                          ALU.min, ALU.max)
                        dv_t = dv_mark()
                        GPS.wait_ge(DV.sem, dv_t)
                        mo = sl_issue(GPS.dma_start(
                            out=wint[r * 128:(r + 1) * 128, :],
                            in_=ptb[r % 2][:, 0:cw]), sGPP, r)
                        sl_wait(GPS, sGPP, mo)
                        mi = sl_issue(GPS.dma_start(
                            out=pscr[:, 0:cw],
                            in_=wint[r * 128:(r + 1) * 128, :]), sGPP, r)
                        gp_c[r] = mi
                        sl_wait(VEC, sGPP, mi)
                        sl_wait(VEC, sWQ, st_q.get(qsi - 2))
                        VEC.scalar_tensor_tensor(pca[:, 0:cw], pscr[:, 0:cw],
                                                 0.5, ptb[r % 2][:, 0:cw],
                                                 ALU.add, ALU.is_lt)
                        VEC.scalar_tensor_tensor(pcb[:, 0:cw], pscr[:, 0:cw],
                                                 -0.5, ptb[r % 2][:, 0:cw],
                                                 ALU.add, ALU.is_ge)
                        VEC.tensor_sub(pca[:, 0:cw], pca[:, 0:cw], pcb[:, 0:cw])
                        VEC.tensor_add(pqb[qsi % 2][:, 0:cw], pca[:, 0:cw],
                                       pscr[:, 0:cw])
                        dv_q[r] = dv_mark()
                        SYN.wait_ge(DV.sem, dv_q[r])
                        st_q[qsi] = sl_issue(SYN.dma_start(
                            out=wdst[r * 128:(r + 1) * 128, :],
                            in_=pqb[qsi % 2][:, 0:cw]), sWQ, qsi)
                        qsi += 1
                    w_done[wdst is wg_s and "g" or (wdst is wu_s and "u" or "d")] \
                        = dict((i, c.n) for i, c in enumerate(sWQ))
                prep_dv = DV.n
                prep_sc = SC.n
                prep_gp = GP.n

            with (nc.sbuf_tensor("xg0", [128, D], F32) as xg0,
                  nc.sbuf_tensor("xg1", [128, D], F32) as xg1,
                  nc.sbuf_tensor("xng0", [128, D], BF16) as xng0,
                  nc.sbuf_tensor("xng1", [128, D], BF16) as xng1,
                  nc.sbuf_tensor("sq_scr", [128, D], F32) as sq_scr,
                  nc.sbuf_tensor("lncol", [128, 4], F32) as lncol):
                xgs = [xg0, xg1]
                xngs = [xng0, xng1]
                SYN.wait_ge(DV.sem, prep_dv)
                SYN.wait_ge(SC.sem, prep_sc)
                sl_wait_all(SYN, sWQ)
                sl_wait_all(SYN, sGPP)
                xn_dv = {}
                xn_st = {}
                for tg in range(NG):
                    SYN.wait_ge(DV.sem, xn_dv.get(tg - 2, prep_dv))
                    mx = sl_issue(SYN.dma_start(
                        out=xgs[tg % 2][:, :],
                        in_=x_in[tg * TG:(tg + 1) * TG, :]), sXA, tg)
                    xgt, xnt = xgs[tg % 2], xngs[tg % 2]
                    SCA.wait_ge(DV.sem, xn_dv.get(tg - 1, prep_dv))
                    sl_wait(SCA, sXA, mx)
                    SCA.activation(sq_scr[:, :], xgt[:, :], ACTF.Square,
                                   accum_out=lncol[:, 1:2])
                    sc_sq = sc_mark()
                    sl_wait(VEC, sXA, mx)
                    sl_wait(VEC, sXN, xn_st.get(tg - 2))
                    VEC.reduce_sum(lncol[:, 0:1], xgt[:, :], AXX)
                    VEC.tensor_scalar_mul(lncol[:, 0:1], lncol[:, 0:1], 1.0 / D)
                    VEC.tensor_mul(lncol[:, 2:3], lncol[:, 0:1], lncol[:, 0:1])
                    VEC.wait_ge(SC.sem, sc_sq)
                    VEC.scalar_tensor_tensor(lncol[:, 3:4], lncol[:, 1:2],
                                             1.0 / D, lncol[:, 2:3],
                                             ALU.mult, ALU.subtract)
                    dv_var = dv_mark()
                    SCA.wait_ge(DV.sem, dv_var)
                    SCA.activation(lncol[:, 2:3], lncol[:, 3:4], ACTF.Ln,
                                   bias=EPS)
                    SCA.activation(lncol[:, 3:4], lncol[:, 2:3], ACTF.Exp,
                                   scale=-0.5)
                    sc_rs = sc_mark()
                    VEC.wait_ge(SC.sem, sc_rs)
                    VEC.tensor_scalar(xnt[:, :], xgt[:, :], lncol[:, 0:1],
                                      lncol[:, 3:4], ALU.subtract, ALU.mult)
                    xn_dv[tg] = dv_mark()
                    SYN.wait_ge(DV.sem, xn_dv[tg])
                    xn_st[tg] = sl_issue(SYN.dma_start(
                        out=xn_s[tg * TG:(tg + 1) * TG, :],
                        in_=xnt[:, :]), sXN, tg)

            a_dv = DV.n
            a_sc = SC.n

            sl_wait_all(SYN, sXN)
            SYN.wait_ge(DV.sem, a_dv)
            SYN.wait_ge(SC.sem, a_sc)
            for dc in range(DC):
                SYN.dma_start(out=xnF[:, dc * L:(dc + 1) * L],
                              in_=xn_s[:, dc * 128:(dc + 1) * 128],
                              transpose=True).then_inc(sXF.sem, 16)
                sXF.inc(16)
            xnF_ld = sXF.n

            with nc.sbuf_tensor("xssmF", [128, EC * L], BF16) as xssmF:

                with (nc.sbuf_tensor("wbuf", [128, 16 * 512], BF16,
                                     is_transpose_dest=True) as wbuf,
                      nc.sbuf_tensor("szev0", [128, 512], BF16) as szev0,
                      nc.sbuf_tensor("szev1", [128, 512], BF16) as szev1,
                      nc.sbuf_tensor("wbc", [128, EC * 32], BF16,
                                     is_transpose_dest=True) as wbc,
                      nc.sbuf_tensor("dtev", [128, 8 * 512], F32) as dtev):
                    szevs = [szev0, szev1]
                    wslot = lambda k: wbuf[:, k * 512:(k + 1) * 512]

                    mg_pe_done = {}
                    ev_sc = {}
                    evg = 0
                    szev_free = {}
                    szi = 0
                    for mg in range(8):
                        for k in range(DC):
                            SYN.wait_ge(PREW[0].sem, 16)
                            if mg > 0:
                                SYN.wait_ge(PE.sem, mg_pe_done[mg - 1])
                            SYN.dma_start(
                                out=wslot(k),
                                in_=win_s[mg * 512:(mg + 1) * 512,
                                          k * 128:(k + 1) * 128],
                                transpose=True).then_inc(WST.sem, 16)
                            WST.inc(16)
                        blk_rd = WST.n
                        for fc in range(4):
                            TEN.wait_ge(WST.sem, blk_rd)
                            if mg == 0 and fc == 0:
                                TEN.wait_ge(sXF.sem, xnF_ld)
                            TEN.wait_ge(SC.sem, ev_sc.get(evg - 1, 0))
                            for k in range(DC):
                                for m in range(4):
                                    TEN.matmul(
                                        big_ps[m][:, :],
                                        wslot(k)[:, m * 128:(m + 1) * 128],
                                        xnF[:, k * L + fc * 512:
                                            k * L + fc * 512 + 512],
                                        start=(k == 0), stop=(k == DC - 1))
                            pe_mm = pe_mark()
                            SCA.wait_ge(PE.sem, pe_mm)
                            for m in range(4):
                                mt = mg * 4 + m
                                if mt < EC:
                                    SCA.copy(
                                        xssmF[:, mt * L + fc * 512:
                                              mt * L + fc * 512 + 512],
                                        big_ps[m][:, :])
                                else:
                                    zi = mt - EC
                                    slot = szi % 2
                                    sl = szevs[slot]
                                    sl_wait(SCA, sSZS, szev_free.get(slot))
                                    SCA.activation(sl[:, :], big_ps[m][:, :],
                                                   ACTF.Silu)
                                    scm = sc_mark()
                                    SYN.wait_ge(SC.sem, scm)
                                    szev_free[slot] = sl_issue(SYN.dma_start(
                                        out=sz_s[zi, :, fc * 512:fc * 512 + 512],
                                        in_=sl[:, :]), sSZS, szi)
                                    szi += 1
                            ev_sc[evg] = sc_mark()
                            evg += 1
                        mg_pe_done[mg] = PE.n
                    c_done_sc = SC.n

                    dtev_free = {}
                    dti = 0
                    dt_last_mark = None
                    for mg in range(4):
                        for k in range(EC):
                            SYN.wait_ge(PREW[1].sem, 16)
                            if mg == 0 and k == 0:
                                SYN.wait_ge(PE.sem, mg_pe_done[7])
                            elif k == 0:
                                SYN.wait_ge(PE.sem, mg_pe_done[("d", mg - 1)])
                            SYN.dma_start(
                                out=wslot(k),
                                in_=wdt_s[mg * 512:(mg + 1) * 512,
                                          k * 128:(k + 1) * 128],
                                transpose=True).then_inc(WST.sem, 16)
                            WST.inc(16)
                        blk_rd = WST.n
                        for fc in range(4):
                            TEN.wait_ge(WST.sem, blk_rd)
                            TEN.wait_ge(SC.sem, ev_sc.get(evg - 1, 0))
                            for k in range(EC):
                                for m in range(4):
                                    TEN.matmul(
                                        big_ps[m][:, :],
                                        wslot(k)[:, m * 128:(m + 1) * 128],
                                        xssmF[:, k * L + fc * 512:
                                              k * L + fc * 512 + 512],
                                        start=(k == 0), stop=(k == EC - 1))
                            pe_mm = pe_mark()
                            SCA.wait_ge(PE.sem, pe_mm)
                            for m in range(4):
                                ec = mg * 4 + m
                                sl_wait(SCA, sDTS, dtev_free.get(m))
                                e_sl = dtev[:, m * 512:(m + 1) * 512]
                                d_sl = dtev[:, (4 + m) * 512:(5 + m) * 512]
                                SCA.activation(e_sl, big_ps[m][:, :], ACTF.Exp,
                                               bias=bdt_all[:, ec:ec + 1])
                                SCA.activation(d_sl, e_sl, ACTF.Ln, bias=1.0)
                                scm = sc_mark()
                                SYN.wait_ge(SC.sem, scm)
                                dtev_free[m] = sl_issue(SYN.dma_start(
                                    out=dt_s[ec, :, fc * 512:fc * 512 + 512],
                                    in_=d_sl), sDTS, dti)
                                dti += 1
                            ev_sc[evg] = sc_mark()
                            evg += 1
                        mg_pe_done[("d", mg)] = PE.n

                    for k in range(EC):
                        SYN.wait_ge(PREW[3].sem, 16)
                        SYN.wait_ge(PREW[4].sem, 16)
                        SYN.dma_start(out=wbc[:, k * 32:k * 32 + 16],
                                      in_=wb_s[:, k * 128:(k + 1) * 128],
                                      transpose=True).then_inc(sBC.sem, 16)
                        sBC.inc(16)
                        SYN.dma_start(out=wbc[:, k * 32 + 16:k * 32 + 32],
                                      in_=wc_s[:, k * 128:(k + 1) * 128],
                                      transpose=True).then_inc(sBC.sem, 16)
                        sBC.inc(16)
                    bc_rd = sBC.n
                    bc_ev = {}
                    for fc in range(4):
                        TEN.wait_ge(sBC.sem, bc_rd)
                        TEN.wait_ge(SC.sem, bc_ev.get(fc - 1, 0))
                        TEN.wait_ge(DV.sem, prep_dv)
                        for k in range(EC):
                            TEN.matmul(
                                mid_ps[0][0:32, :],
                                wbc[:, k * 32:k * 32 + 32],
                                xssmF[:, k * L + fc * 512:k * L + fc * 512 + 512],
                                start=(k == 0), stop=(k == EC - 1))
                        pe_bc2 = pe_mark()
                        SCA.wait_ge(PE.sem, pe_bc2)
                        SCA.copy(BC_sb[0:32, fc * 512:fc * 512 + 512],
                                 mid_ps[0][0:32, :])
                        bc_ev[fc] = sc_mark()
                    SYN.wait_ge(SC.sem, bc_ev[3])
                    SYN.dma_start(out=bc_d[:, :], in_=BC_sb[0:32, :]
                                  ).then_inc(sBCD.sem, 16)
                    sBCD.inc(16)
                    d_done_sc = SC.n

                with (nc.sbuf_tensor("dtb", [128, 3 * 512], F32) as dtb,
                      nc.sbuf_tensor("szb", [128, 3 * 512], BF16) as szb,
                      nc.sbuf_tensor("wbs", [128, 2 * 512], F32) as wbs,
                      nc.sbuf_tensor("abz", [128, 2 * 512], F32) as abz,
                      nc.sbuf_tensor("ubz", [128, 2 * 512], F32) as ubz,
                      nc.sbuf_tensor("Hbz", [128, 2 * 512], F32) as Hbz,
                      nc.sbuf_tensor("hcz", [128, 2 * 512], F32) as hcz,
                      nc.sbuf_tensor("yccz", [128, 2 * 512], F32) as yccz,
                      nc.sbuf_tensor("Bb", [128, N * 512], BF16) as Bb,
                      nc.sbuf_tensor("Cb", [128, N * 512], BF16) as Cb,
                      nc.sbuf_tensor("yinsl", [128, 2 * 512], BF16) as yinsl):
                    sl = lambda t, i, w=512: t[:, i * w:(i + 1) * w]
                    sl_wait_all(SYN, sDTS)
                    sl_wait_all(SYN, sSZS)
                    dt_ld = {}
                    sz_ld = {}
                    dt_dv_free = {}
                    sc_a = {}
                    ab_free = {}
                    hc_rdy = {}
                    gp_n = {}
                    y_fold = {}
                    yin_st = {}
                    bb_ld = {}
                    for ch in range(NCH):
                        prev_scan_dv = y_fold.get(ch * EC - 1, 0)
                        SYN.wait_ge(sBCD.sem, sBCD.n)
                        SYN.wait_ge(DV.sem, prev_scan_dv)
                        for n in range(N):
                            for srow, dst in ((n, Bb), (N + n, Cb)):
                                SYN.dma_start(
                                    out=sl(dst, n),
                                    in_=bc_d[srow:srow + 1,
                                             ch * 512:(ch + 1) * 512]
                                    .partition_broadcast(128)
                                    ).then_inc(sBB.sem, 16)
                                sBB.inc(16)
                        bb_ld[ch] = sBB.n

                        for c in range(EC):
                            ci = ch * EC + c
                            s3 = ci % 3
                            s2 = ci % 2
                            SYN.wait_ge(DV.sem, dt_dv_free.get(ci - 3, 0))
                            SYN.wait_ge(SC.sem, sc_a.get(ci - 3, 0))
                            dt_ld[ci] = sl_issue(SYN.dma_start(
                                out=sl(dtb, s3),
                                in_=dt_s[c, :, ch * 512:ch * 512 + 512]),
                                sDT, ci)
                            sz_ld[ci] = sl_issue(SYN.dma_start(
                                out=sl(szb, s3),
                                in_=sz_s[c, :, ch * 512:ch * 512 + 512]),
                                sSZ, ci)
                            sl_wait(VEC, sDT, dt_ld[ci])
                            if c == 0:
                                VEC.wait_ge(sBB.sem, bb_ld[ch])
                            VEC.tensor_mul(
                                sl(wbs, s2), sl(dtb, s3),
                                xssmF[:, c * L + ch * 512:c * L + ch * 512 + 512])
                            w_rdy = dv_mark()
                            for n in range(N):
                                cn = c * N + n
                                p2 = n % 2
                                sl_wait(SCA, sDT, dt_ld[ci])
                                SCA.wait_ge(DV.sem,
                                            ab_free.get(p2, 0))
                                SCA.activation(sl(abz, p2), sl(dtb, s3),
                                               ACTF.Exp,
                                               scale=A_all[:, cn:cn + 1])
                                sc_an = sc_mark()
                                VEC.wait_ge(SC.sem, sc_an)
                                VEC.wait_ge(GP.sem,
                                            gp_n.get((ci, n - 2),
                                                     gp_n.get((ci - 1,
                                                               N - 2 + n), 0)))
                                VEC.tensor_mul(sl(ubz, p2), sl(wbs, s2),
                                               sl(Bb, n))
                                VEC.tensor_tensor_scan(
                                    sl(Hbz, p2), sl(abz, p2), sl(ubz, p2),
                                    h_state[:, cn:cn + 1], ALU.mult, ALU.add)
                                VEC.tensor_mul(sl(hcz, p2), sl(Hbz, p2),
                                               sl(Cb, n))
                                hc_rdy[(ci, n)] = dv_mark()
                                ab_free[p2] = hc_rdy[(ci, n)]
                                GPS.wait_ge(DV.sem, hc_rdy[(ci, n)])
                                if n == 0:
                                    GPS.wait_ge(DV.sem, y_fold.get(ci - 2, 0))
                                    GPS.tensor_copy(sl(yccz, ci % 2),
                                                    sl(hcz, p2))
                                else:
                                    GPS.tensor_add(sl(yccz, ci % 2),
                                                   sl(yccz, ci % 2),
                                                   sl(hcz, p2))
                                GPS.tensor_copy(h_state[:, cn:cn + 1],
                                                sl(Hbz, p2)[:, 511:512])
                                gp_n[(ci, n)] = gp_mark()
                            sc_a[ci] = SC.n
                            VEC.wait_ge(GP.sem, gp_n[(ci, N - 1)])
                            sl_wait(VEC, sSZ, sz_ld[ci])
                            sl_wait(VEC, sYIN, yin_st.get(ci - 2))
                            VEC.scalar_tensor_tensor(
                                sl(wbs, s2),
                                xssmF[:, c * L + ch * 512:c * L + ch * 512 + 512],
                                D_all[:, c:c + 1], sl(yccz, ci % 2),
                                ALU.mult, ALU.add)
                            VEC.scalar_tensor_tensor(
                                sl(yinsl, ci % 2), sl(wbs, s2), 1.0,
                                sl(szb, s3), ALU.mult, ALU.mult)
                            y_fold[ci] = dv_mark()
                            dt_dv_free[ci] = y_fold[ci]
                            SYN.wait_ge(DV.sem, y_fold[ci])
                            yin_st[ci] = sl_issue(SYN.dma_start(
                                out=yin_s[c, :, ch * 512:ch * 512 + 512],
                                in_=sl(yinsl, ci % 2)), sYIN, ci)
                    e_done_sc = SC.n
                    e_done_dv = DV.n
                    e_done_gp = GP.n

                with (nc.sbuf_tensor("wo_blk", [128, EC * 1024], BF16,
                                     is_transpose_dest=True) as wo_blk,
                      nc.sbuf_tensor("yb", [128, EC * 512], BF16) as yb,
                      nc.sbuf_tensor("fxg0", [128, D], F32) as fxg0,
                      nc.sbuf_tensor("fxg1", [128, D], F32) as fxg1,
                      nc.sbuf_tensor("fx10", [128, D], F32) as fx10,
                      nc.sbuf_tensor("fx11", [128, D], F32) as fx11,
                      nc.sbuf_tensor("fiq0", [128, D], BF16) as fiq0,
                      nc.sbuf_tensor("fiq1", [128, D], BF16) as fiq1,
                      nc.sbuf_tensor("qta", [128, D], F32) as qta,
                      nc.sbuf_tensor("qtb", [128, D], F32) as qtb,
                      nc.sbuf_tensor("qtc", [128, D], F32) as qtc,
                      nc.sbuf_tensor("sq2", [128, D], F32) as sq2,
                      nc.sbuf_tensor("lnc2", [128, 4], F32) as lnc2):
                    fxg = [fxg0, fxg1]
                    fx1 = [fx10, fx11]
                    fiq = [fiq0, fiq1]
                    SYN.wait_ge(DV.sem, e_done_dv)
                    SYN.wait_ge(SC.sem, e_done_sc)
                    SYN.wait_ge(GP.sem, e_done_gp)
                    sl_wait_all(SYN, sYIN)
                    sl_wait_all(SYN, sDT)
                    sl_wait_all(SYN, sSZ)
                    for ec in range(EC):
                        for o in range(2):
                            SYN.wait_ge(PREW[2].sem, 16)
                            SYN.dma_start(
                                out=wo_blk[:, (ec * 2 + o) * 512:
                                           (ec * 2 + o) * 512 + 512],
                                in_=wo_s[o * 512:(o + 1) * 512,
                                         ec * 128:(ec + 1) * 128],
                                transpose=True).then_inc(WST.sem, 16)
                            WST.inc(16)
                    wo_rd = WST.n
                    dv_x1m = {}
                    sc_sq2m = {}
                    gp_iq = {}
                    iq_st = {}
                    x1_st = {}
                    fev = {}
                    pe_tq = {}
                    for tq in range(NCH):
                        sl_wait_all(SYN, sYIN)
                        if tq > 0:
                            SYN.wait_ge(PE.sem, pe_tq[tq - 1])
                        for ec in range(EC):
                            SYN.dma_start(
                                out=yb[:, ec * 512:(ec + 1) * 512],
                                in_=yin_s[ec, :, tq * 512:(tq + 1) * 512]
                                ).then_inc(sYB.sem, 16)
                            sYB.inc(16)
                        yb_ld = sYB.n
                        for th in range(2):
                            TEN.wait_ge(WST.sem, wo_rd)
                            TEN.wait_ge(sYB.sem, yb_ld)
                            TEN.wait_ge(SC.sem, e_done_sc)
                            TEN.wait_ge(DV.sem,
                                        fev.get((tq, th - 1),
                                                fev.get((tq - 1, 1), 0)))
                            for ec in range(EC):
                                for t2 in range(2):
                                    tt = th * 2 + t2
                                    for o in range(2):
                                        TEN.matmul(
                                            mid_ps[t2 * 2 + o][:, :],
                                            yb[:, ec * 512 + tt * 128:
                                               ec * 512 + tt * 128 + 128],
                                            wo_blk[:, (ec * 2 + o) * 512:
                                                   (ec * 2 + o) * 512 + 512],
                                            start=(ec == 0),
                                            stop=(ec == EC - 1))
                            pe_o = pe_mark()
                            pe_tq[tq] = pe_o
                            for t2 in range(2):
                                tg = tq * 4 + th * 2 + t2
                                par = tg % 2
                                SYN.wait_ge(DV.sem, dv_x1m.get(tg - 2, 0))
                                mfx = sl_issue(SYN.dma_start(
                                    out=fxg[par][:, :],
                                    in_=x_in[tg * TG:(tg + 1) * TG, :]),
                                    sFX, tg)
                                VEC.wait_ge(PE.sem, pe_o)
                                sl_wait(VEC, sFX, mfx)
                                sl_wait(VEC, sX1S, x1_st.get(tg - 2))
                                VEC.wait_ge(SC.sem, sc_sq2m.get(tg - 2, 0))
                                for o in range(2):
                                    VEC.scalar_tensor_tensor(
                                        fx1[par][:, o * 512:o * 512 + 512],
                                        mid_ps[t2 * 2 + o][:, :], cols[:, 0:1],
                                        fxg[par][:, o * 512:o * 512 + 512],
                                        ALU.mult, ALU.add)
                                dv_x1 = dv_mark()
                                dv_x1m[tg] = dv_x1
                                SYN.wait_ge(DV.sem, dv_x1)
                                x1_st[tg] = sl_issue(SYN.dma_start(
                                    out=x1_s[tg * TG:(tg + 1) * TG, :],
                                    in_=fx1[par][:, :]), sX1S, tg)
                                SCA.wait_ge(DV.sem, dv_x1)
                                SCA.activation(sq2[:, :], fx1[par][:, :],
                                               ACTF.Square,
                                               accum_out=lnc2[:, 1:2])
                                sc_sq2 = sc_mark()
                                sc_sq2m[tg] = sc_sq2
                                VEC.reduce_sum(lnc2[:, 0:1], fx1[par][:, :],
                                               AXX)
                                VEC.tensor_scalar_mul(lnc2[:, 0:1],
                                                      lnc2[:, 0:1], 1.0 / D)
                                VEC.tensor_mul(lnc2[:, 2:3], lnc2[:, 0:1],
                                               lnc2[:, 0:1])
                                VEC.wait_ge(SC.sem, sc_sq2)
                                VEC.scalar_tensor_tensor(
                                    lnc2[:, 3:4], lnc2[:, 1:2], 1.0 / D,
                                    lnc2[:, 2:3], ALU.mult, ALU.subtract)
                                dv_v2 = dv_mark()
                                SCA.wait_ge(DV.sem, dv_v2)
                                SCA.activation(lnc2[:, 2:3], lnc2[:, 3:4],
                                               ACTF.Ln, bias=EPS)
                                SCA.activation(lnc2[:, 3:4], lnc2[:, 2:3],
                                               ACTF.Exp, scale=-0.5)
                                sc_r2 = sc_mark()
                                VEC.wait_ge(SC.sem, sc_r2)
                                sl_wait(VEC, sIQS, iq_st.get(tg - 2))
                                sl_wait(VEC, sGPF, gp_iq.get(tg - 1))
                                VEC.tensor_scalar(qta[:, :], fx1[par][:, :],
                                                  lnc2[:, 0:1], lnc2[:, 3:4],
                                                  ALU.subtract, ALU.mult)
                                VEC.tensor_reduce(lnc2[:, 0:1], qta[:, :], AXX,
                                                  ALU.max,
                                                  apply_absolute_value=True)
                                VEC.tensor_scalar_max(lnc2[:, 0:1],
                                                      lnc2[:, 0:1], 1e-5)
                                VEC.tensor_scalar_mul(stinv[:, tg:tg + 1],
                                                      lnc2[:, 0:1], 1.0 / 127.0)
                                VEC.reciprocal(lnc2[:, 1:2], lnc2[:, 0:1])
                                VEC.tensor_scalar_mul(lnc2[:, 1:2],
                                                      lnc2[:, 1:2], 127.0)
                                VEC.tensor_scalar(qtb[:, :], qta[:, :],
                                                  lnc2[:, 1:2], None, ALU.mult)
                                dv_tq = dv_mark()
                                GPS.wait_ge(DV.sem, dv_tq)
                                mgo = sl_issue(GPS.dma_start(
                                    out=iqi_s[tg * TG:(tg + 1) * TG, :],
                                    in_=qtb[:, :]), sGPF, 0)
                                sl_wait(GPS, sGPF, mgo)
                                mgi = sl_issue(GPS.dma_start(
                                    out=qta[:, :],
                                    in_=iqi_s[tg * TG:(tg + 1) * TG, :]),
                                    sGPF, 0)
                                gp_iq[tg] = mgi
                                sl_wait(VEC, sGPF, mgi)
                                VEC.scalar_tensor_tensor(
                                    sq2[:, :], qta[:, :], 0.5, qtb[:, :],
                                    ALU.add, ALU.is_lt)
                                VEC.scalar_tensor_tensor(
                                    qtc[:, :], qta[:, :], -0.5, qtb[:, :],
                                    ALU.add, ALU.is_ge)
                                VEC.tensor_sub(sq2[:, :], sq2[:, :], qtc[:, :])
                                VEC.tensor_add(fiq[par][:, :], sq2[:, :],
                                               qta[:, :])
                                VEC.tensor_mul(sgam[:, tg:tg + 1],
                                               stinv[:, tg:tg + 1],
                                               cols[:, 4:5])
                                VEC.tensor_mul(suam[:, tg:tg + 1],
                                               stinv[:, tg:tg + 1],
                                               cols[:, 5:6])
                                dv_iq = dv_mark()
                                SYN.wait_ge(DV.sem, dv_iq)
                                iq_st[tg] = sl_issue(SYN.dma_start(
                                    out=iq_s[tg * TG:(tg + 1) * TG, :],
                                    in_=fiq[par][:, :]), sIQS, tg)
                            fev[(tq, th)] = DV.n
                    f_done_dv = DV.n
                    f_done_pe = PE.n
                    f_done_sc = SC.n

            sl_wait_all(SYN, sIQS)
            sl_wait_all(SYN, sX1S)
            sl_wait_all(SYN, sFX)
            sl_wait_all(SYN, sGPF)
            SYN.wait_ge(PE.sem, f_done_pe)
            SYN.wait_ge(DV.sem, f_done_dv)
            SYN.wait_ge(SC.sem, f_done_sc)
            for dc in range(DC):
                SYN.dma_start(out=xnF[:, dc * L:(dc + 1) * L],
                              in_=iq_s[:, dc * 128:(dc + 1) * 128],
                              transpose=True).then_inc(sXF.sem, 16)
                sXF.inc(16)
            iqF_ld = sXF.n

            with nc.sbuf_tensor("wgb", [128, 8 * II], BF16,
                                is_transpose_dest=True) as wgb:
                with (nc.sbuf_tensor("sgb0", [128, II], F32) as sgb0,
                      nc.sbuf_tensor("sgb1", [128, II], F32) as sgb1,
                      nc.sbuf_tensor("prod0", [128, II], F32) as prod0,
                      nc.sbuf_tensor("prod1", [128, II], F32) as prod1,
                      nc.sbuf_tensor("u2b0", [128, 512], F32) as u2b0,
                      nc.sbuf_tensor("u2b1", [128, 512], F32) as u2b1,
                      nc.sbuf_tensor("qva", [128, 512], F32) as qva,
                      nc.sbuf_tensor("qvb", [128, 512], F32) as qvb,
                      nc.sbuf_tensor("qtt", [128, II], F32) as qtt,
                      nc.sbuf_tensor("q2sl", [128, 2 * 512], BF16) as q2sl,
                      nc.sbuf_tensor("qcl", [128, 4], F32) as qcl):
                    sgb = [sgb0, sgb1]
                    prodb = [prod0, prod1]
                    u2b = [u2b0, u2b1]
                    pe_hblk = {}
                    q2_st = {}
                    gp_q2 = {}
                    q2dv = {}
                    prod_dv = {}
                    q2i = 0
                    for hc in range(8):
                        for k in range(DC):
                            if hc == 0 and k == 0:
                                sl_wait_all(SYN, sWQ)
                                SYN.wait_ge(DV.sem, f_done_dv)
                            if k == 0 and hc > 0:
                                SYN.wait_ge(PE.sem, pe_hblk[("u", hc - 1)])
                            SYN.dma_start(out=wgb[:, k * II:(k + 1) * II],
                                          in_=wg_s[:, k * 128:(k + 1) * 128],
                                          transpose=True).then_inc(WST.sem, 16)
                            WST.inc(16)
                        wg_rd = WST.n
                        sgev = {}
                        for o in range(8):
                            for tg2 in range(2):
                                tg = hc * 2 + tg2
                                TEN.wait_ge(WST.sem, wg_rd)
                                if hc == 0 and o == 0 and tg2 == 0:
                                    TEN.wait_ge(sXF.sem, iqF_ld)
                                TEN.wait_ge(SC.sem, sgev.get((o - 1, tg2), 0))
                                for k in range(DC):
                                    TEN.matmul(
                                        mid_ps[tg2][:, :],
                                        xnF[:, k * L + tg * 128:
                                            k * L + tg * 128 + 128],
                                        wgb[:, k * II + o * 512:
                                            k * II + o * 512 + 512],
                                        start=(k == 0), stop=(k == DC - 1))
                                pe_g2 = pe_mark()
                                SCA.wait_ge(PE.sem, pe_g2)
                                SCA.wait_ge(DV.sem, q2dv.get((hc - 1, tg2), 0))
                                SCA.activation(
                                    sgb[tg2][:, o * 512:o * 512 + 512],
                                    mid_ps[tg2][:, :], ACTF.Silu,
                                    scale=sgam[:, tg:tg + 1])
                                sgev[(o, tg2)] = sc_mark()
                        pe_hblk[("g", hc)] = PE.n
                        for k in range(DC):
                            if k == 0:
                                SYN.wait_ge(PE.sem, pe_hblk[("g", hc)])
                            SYN.dma_start(out=wgb[:, k * II:(k + 1) * II],
                                          in_=wu_s[:, k * 128:(k + 1) * 128],
                                          transpose=True).then_inc(WST.sem, 16)
                            WST.inc(16)
                        wu_rd = WST.n
                        uev = {}
                        for o in range(8):
                            for tg2 in range(2):
                                tg = hc * 2 + tg2
                                TEN.wait_ge(WST.sem, wu_rd)
                                TEN.wait_ge(SC.sem, uev.get((o - 1, tg2), 0))
                                for k in range(DC):
                                    TEN.matmul(
                                        mid_ps[2 + tg2][:, :],
                                        xnF[:, k * L + tg * 128:
                                            k * L + tg * 128 + 128],
                                        wgb[:, k * II + o * 512:
                                            k * II + o * 512 + 512],
                                        start=(k == 0), stop=(k == DC - 1))
                                pe_u2 = pe_mark()
                                par = tg2
                                SCA.wait_ge(PE.sem, pe_u2)
                                SCA.wait_ge(DV.sem,
                                            prod_dv.get((hc, o - 1, tg2), 0))
                                SCA.activation(u2b[par][:, :],
                                               mid_ps[2 + tg2][:, :],
                                               ACTF.Copy,
                                               scale=suam[:, tg:tg + 1])
                                uev[(o, tg2)] = sc_mark()
                                VEC.wait_ge(SC.sem, uev[(o, tg2)])
                                VEC.tensor_mul(
                                    prodb[tg2][:, o * 512:o * 512 + 512],
                                    sgb[tg2][:, o * 512:o * 512 + 512],
                                    u2b[par][:, :])
                                prod_dv[(hc, o, tg2)] = dv_mark()
                        pe_hblk[("u", hc)] = PE.n
                        for tg2 in range(2):
                            tg = hc * 2 + tg2
                            VEC.wait_ge(DV.sem, prod_dv[(hc, 7, tg2)])
                            sl_wait(VEC, sGPH, gp_q2.get(tg - 1))
                            VEC.tensor_reduce(qcl[:, 0:1], prodb[tg2][:, :],
                                              AXX, ALU.max,
                                              apply_absolute_value=True)
                            VEC.tensor_scalar_max(qcl[:, 0:1], qcl[:, 0:1],
                                                  1e-5)
                            VEC.tensor_scalar_mul(qcl[:, 1:2], qcl[:, 0:1],
                                                  1.0 / 127.0)
                            VEC.tensor_mul(beta[:, tg:tg + 1], qcl[:, 1:2],
                                           cols[:, 6:7])
                            VEC.reciprocal(qcl[:, 2:3], qcl[:, 0:1])
                            VEC.tensor_scalar_mul(qcl[:, 2:3], qcl[:, 2:3],
                                                  127.0)
                            VEC.tensor_scalar(qtt[:, :], prodb[tg2][:, :],
                                              qcl[:, 2:3], None, ALU.mult)
                            dv_tq2 = dv_mark()
                            GPS.wait_ge(DV.sem, dv_tq2)
                            mgo = sl_issue(GPS.dma_start(
                                out=q2i_s[tg * TG:(tg + 1) * TG, :],
                                in_=qtt[:, :]), sGPH, 0)
                            sl_wait(GPS, sGPH, mgo)
                            mgi = sl_issue(GPS.dma_start(
                                out=sgb[tg2][:, :],
                                in_=q2i_s[tg * TG:(tg + 1) * TG, :]),
                                sGPH, 0)
                            gp_q2[tg] = mgi
                            sl_wait(VEC, sGPH, mgi)
                            for o in range(8):
                                r0 = sgb[tg2][:, o * 512:o * 512 + 512]
                                tt_ = qtt[:, o * 512:o * 512 + 512]
                                sl_wait(VEC, sQ2S, q2_st.get(q2i - 2))
                                VEC.scalar_tensor_tensor(
                                    qva[:, :], r0, 0.5, tt_, ALU.add, ALU.is_lt)
                                VEC.scalar_tensor_tensor(
                                    qvb[:, :], r0, -0.5, tt_, ALU.add,
                                    ALU.is_ge)
                                VEC.tensor_sub(qva[:, :], qva[:, :], qvb[:, :])
                                VEC.tensor_add(
                                    q2sl[:, (q2i % 2) * 512:
                                         (q2i % 2) * 512 + 512],
                                    qva[:, :], r0)
                                dv_q2 = dv_mark()
                                SYN.wait_ge(DV.sem, dv_q2)
                                q2_st[q2i] = sl_issue(SYN.dma_start(
                                    out=q2_s[tg * TG:(tg + 1) * TG,
                                             o * 512:o * 512 + 512],
                                    in_=q2sl[:, (q2i % 2) * 512:
                                             (q2i % 2) * 512 + 512]),
                                    sQ2S, q2i)
                                q2i += 1
                            q2dv[(hc, tg2)] = DV.n
                    h_done_pe = PE.n
                    h_done_sc = SC.n
                    h_done_dv = DV.n

                with (nc.sbuf_tensor("q2F0", [128, 512], BF16,
                                     is_transpose_dest=True) as q2F0,
                      nc.sbuf_tensor("q2F1", [128, 512], BF16,
                                     is_transpose_dest=True) as q2F1,
                      nc.sbuf_tensor("q2F2", [128, 512], BF16,
                                     is_transpose_dest=True) as q2F2,
                      nc.sbuf_tensor("q2F3", [128, 512], BF16,
                                     is_transpose_dest=True) as q2F3,
                      nc.sbuf_tensor("ob0", [128, D], F32) as ob0,
                      nc.sbuf_tensor("ob1", [128, D], F32) as ob1,
                      nc.sbuf_tensor("ob2", [128, D], F32) as ob2,
                      nc.sbuf_tensor("ob3", [128, D], F32) as ob3,
                      nc.sbuf_tensor("ix0", [128, D], F32) as ix0,
                      nc.sbuf_tensor("ix1_", [128, D], F32) as ix1_,
                      nc.sbuf_tensor("ix2", [128, D], F32) as ix2,
                      nc.sbuf_tensor("ix3", [128, D], F32) as ix3):
                    q2F = [q2F0, q2F1, q2F2, q2F3]
                    outbs = [ob0, ob1, ob2, ob3]
                    ix1 = [ix0, ix1_, ix2, ix3]
                    SYN.wait_ge(PE.sem, h_done_pe)
                    SYN.wait_ge(SC.sem, h_done_sc)
                    SYN.wait_ge(DV.sem, h_done_dv)
                    sl_wait_all(SYN, sQ2S)
                    sl_wait_all(SYN, sGPH)
                    sl_wait_all(SYN, sX1S)
                    for k in range(32):
                        SYN.dma_start(out=wgb[:, k * 1024:(k + 1) * 1024],
                                      in_=wd_s[:, k * 128:(k + 1) * 128],
                                      transpose=True).then_inc(WST.sem, 16)
                        WST.inc(16)
                    wd_rd = WST.n
                    q2f_free = {}
                    q2fi = 0
                    iev = {}
                    x1_ld = {}
                    out_st = {}
                    dv_out = {}
                    for tq in range(NCH):
                        for o in range(2):
                            for k in range(32):
                                SYN.wait_ge(PE.sem,
                                            q2f_free.get(q2fi - 4, 0))
                                mq = sl_issue(SYN.dma_start(
                                    out=q2F[q2fi % 4][:, :],
                                    in_=q2_s[tq * 512:(tq + 1) * 512,
                                             k * 128:(k + 1) * 128],
                                    transpose=True), sQF, q2fi)
                                sl_wait(TEN, sQF, mq)
                                if k == 0:
                                    TEN.wait_ge(WST.sem, wd_rd)
                                    TEN.wait_ge(SC.sem, h_done_sc)
                                    TEN.wait_ge(DV.sem,
                                                iev.get((tq, o - 1),
                                                        iev.get((tq - 1, 1),
                                                                0)))
                                for t2 in range(4):
                                    TEN.matmul(
                                        mid_ps[t2][:, :],
                                        q2F[q2fi % 4][:, t2 * 128:
                                                      t2 * 128 + 128],
                                        wgb[:, k * 1024 + o * 512:
                                            k * 1024 + o * 512 + 512],
                                        start=(k == 0), stop=(k == 31))
                                q2f_free[q2fi] = pe_mark()
                                q2fi += 1
                            pe_dn = PE.n
                            for t2 in range(4):
                                tg = tq * 4 + t2
                                if o == 0:
                                    SYN.wait_ge(DV.sem, dv_out.get(tg - 4, 0))
                                    x1_ld[tg] = sl_issue(SYN.dma_start(
                                        out=ix1[t2][:, :],
                                        in_=x1_s[tg * TG:(tg + 1) * TG, :]),
                                        sXI, t2)
                                VEC.wait_ge(PE.sem, pe_dn)
                                sl_wait(VEC, sXI, x1_ld[tg])
                                if o == 0:
                                    sl_wait(VEC, sOUTS, out_st.get(tg - 4))
                                VEC.scalar_tensor_tensor(
                                    outbs[t2][:, o * 512:o * 512 + 512],
                                    mid_ps[t2][:, :], beta[:, tg:tg + 1],
                                    ix1[t2][:, o * 512:o * 512 + 512],
                                    ALU.mult, ALU.add)
                                dv_o = dv_mark()
                                if o == 1:
                                    dv_out[tg] = dv_o
                                    SYN.wait_ge(DV.sem, dv_o)
                                    out_st[tg] = sl_issue(SYN.dma_start(
                                        out=out[tg * TG:(tg + 1) * TG, :],
                                        in_=outbs[t2][:, :]), sOUTS, tg)
                            iev[(tq, o)] = DV.n
                    sl_wait_all(SYN, sOUTS)

    return nc


def kernel(L0_attn_x, L0_attn_W_in, L0_attn_A_log, L0_attn_W_B, L0_attn_W_C,
           L0_attn_W_dt, L0_attn_b_dt, L0_attn_D, L0_attn_output_gate,
           L0_attn_W_out, L0_ffn_w_gate, L0_ffn_w_up, L0_ffn_w_down):
    xs = np.ascontiguousarray(np.asarray(L0_attn_x, dtype=np.float32))
    weights = {
        "w_in": np.ascontiguousarray(np.asarray(L0_attn_W_in, np.float32)),
        "a_log": np.ascontiguousarray(np.asarray(L0_attn_A_log, np.float32)),
        "w_b": np.ascontiguousarray(np.asarray(L0_attn_W_B, np.float32)),
        "w_c": np.ascontiguousarray(np.asarray(L0_attn_W_C, np.float32)),
        "w_dt": np.ascontiguousarray(np.asarray(L0_attn_W_dt, np.float32)),
        "b_dt": np.ascontiguousarray(np.asarray(L0_attn_b_dt, np.float32)),
        "d_vec": np.ascontiguousarray(np.asarray(L0_attn_D, np.float32)),
        "gate": np.ascontiguousarray(np.asarray(L0_attn_output_gate, np.float32)),
        "w_out": np.ascontiguousarray(np.asarray(L0_attn_W_out, np.float32)),
        "w_g": np.ascontiguousarray(np.asarray(L0_ffn_w_gate, np.float32)),
        "w_u": np.ascontiguousarray(np.asarray(L0_ffn_w_up, np.float32)),
        "w_d": np.ascontiguousarray(np.asarray(L0_ffn_w_down, np.float32)),
    }
    nc = build_program()
    in_maps = []
    for b in range(8):
        m = {"x": np.ascontiguousarray(xs[b])}
        m.update(weights)
        in_maps.append(m)
    res = run_bass_kernel_spmd(nc, in_maps, core_ids=list(range(8)))
    return np.stack([np.asarray(r["out"], dtype=np.float32)
                     for r in res.results], axis=0)


# version 04
# speedup vs baseline: 1.1796x; 1.1796x over previous
import contextlib

import numpy as np

import concourse.bass as bass
import concourse.mybir as mybir
from concourse.bass_utils import run_bass_kernel_spmd

F32 = mybir.dt.float32
BF16 = mybir.dt.bfloat16
I8 = mybir.dt.int8
ALU = mybir.AluOpType
ACTF = mybir.ActivationFunctionType
AXX = mybir.AxisListType.X

L = 2048
D = 1024
E = 2048
N = 16
II = 4096
TG = 128
NG = L // TG
CH = 512
NCH = L // CH
EC = E // 128
DC = D // 128
EPS = 1e-5


class Ctr:
    def __init__(self, sem):
        self.sem = sem
        self.n = 0

    def inc(self, by=1):
        self.n += by
        return self.n


def build_program():
    nc = bass.Bass("TRN2", target_bir_lowering=False)

    x_in = nc.dram_tensor("x", [L, D], F32, kind="ExternalInput").ap()
    w_in = nc.dram_tensor("w_in", [2 * E, D], F32, kind="ExternalInput").ap()
    a_log = nc.dram_tensor("a_log", [E, N], F32, kind="ExternalInput").ap()
    w_b = nc.dram_tensor("w_b", [N, E], F32, kind="ExternalInput").ap()
    w_c = nc.dram_tensor("w_c", [N, E], F32, kind="ExternalInput").ap()
    w_dt = nc.dram_tensor("w_dt", [E, E], F32, kind="ExternalInput").ap()
    b_dt = nc.dram_tensor("b_dt", [E], F32, kind="ExternalInput").ap()
    d_vec = nc.dram_tensor("d_vec", [E], F32, kind="ExternalInput").ap()
    gate = nc.dram_tensor("gate", [1], F32, kind="ExternalInput").ap()
    w_out = nc.dram_tensor("w_out", [D, E], F32, kind="ExternalInput").ap()
    w_g = nc.dram_tensor("w_g", [II, D], F32, kind="ExternalInput").ap()
    w_u = nc.dram_tensor("w_u", [II, D], F32, kind="ExternalInput").ap()
    w_d = nc.dram_tensor("w_d", [D, II], F32, kind="ExternalInput").ap()
    out = nc.dram_tensor("out", [L, D], F32, kind="ExternalOutput").ap()

    win_s = nc.dram_tensor("win_s", [2 * E, D], BF16).ap()
    wdt_s = nc.dram_tensor("wdt_s", [E, E], BF16).ap()
    wo_s = nc.dram_tensor("wo_s", [D, E], BF16).ap()
    wb_s = nc.dram_tensor("wb_s", [N, E], BF16).ap()
    wc_s = nc.dram_tensor("wc_s", [N, E], BF16).ap()
    wg_s = nc.dram_tensor("wg_s", [II, D], BF16).ap()
    wu_s = nc.dram_tensor("wu_s", [II, D], BF16).ap()
    wd_s = nc.dram_tensor("wd_s", [D, II], BF16).ap()
    wgi_s = nc.dram_tensor("wgi_s", [II, D], I8).ap()
    wui_s = nc.dram_tensor("wui_s", [II, D], I8).ap()
    wdi_s = nc.dram_tensor("wdi_s", [D, II], I8).ap()
    iqi_s = nc.dram_tensor("iqi_s", [L, D], I8).ap()
    q2i_s = nc.dram_tensor("q2i_s", [L, II], I8).ap()
    xn_s = nc.dram_tensor("xn_s", [L, D], BF16).ap()
    iq_s = nc.dram_tensor("iq_s", [L, D], BF16).ap()
    q2_s = nc.dram_tensor("q2_s", [L, II], BF16).ap()
    x1_s = nc.dram_tensor("x1_s", [L, D], F32).ap()
    dt_s = nc.dram_tensor("dt_s", [EC, 128, L], F32).ap()
    sz_s = nc.dram_tensor("sz_s", [EC, 128, L], BF16).ap()
    yin_s = nc.dram_tensor("yin_s", [EC, 128, L], BF16).ap()
    bc_d = nc.dram_tensor("bc_d", [32, L], BF16).ap()

    stk = contextlib.ExitStack()
    T = lambda *a, **k: stk.enter_context(nc.sbuf_tensor(*a, **k))
    P = lambda *a, **k: stk.enter_context(nc.psum_tensor(*a, **k))
    SEM = lambda name: stk.enter_context(nc.semaphore(name))

    with stk:
        A_all = T("A_all", [128, EC * N], F32)
        D_all = T("D_all", [128, EC], F32)
        bdt_all = T("bdt_all", [128, EC], F32)
        cols = T("cols", [128, 16], F32)
        ones_f = T("ones_f", [1, 128], F32)
        onescol_f = T("onescol_f", [128, 1], F32)
        h_state = T("h_state", [128, EC * N], F32)
        stinv = T("stinv", [128, NG], F32)
        sgam = T("sgam", [128, NG], F32)
        suam = T("suam", [128, NG], F32)
        beta = T("beta", [128, NG], F32)
        tiny = T("tiny", [128, 8], F32)
        BC_sb = T("BC_sb", [32, L], BF16)
        xnF = T("xnF", [128, DC * L], BF16, is_transpose_dest=True)
        cst = T("cst", [128, 8], F32)

        big_ps = [P(f"big_ps{i}", [128, 512], F32) for i in range(4)]
        mid_ps = [P(f"mid_ps{i}", [128, 512], F32) for i in range(4)]

        PE = Ctr(SEM("pe"))
        SC = Ctr(SEM("sc"))
        DV = Ctr(SEM("dv"))
        GP = Ctr(SEM("gp"))
        WST = Ctr(SEM("wst"))
        WSTp = [Ctr(SEM("wstp0")), Ctr(SEM("wstp1"))]
        sBC = Ctr(SEM("sbc"))
        sXF = Ctr(SEM("sxf"))
        sCON = Ctr(SEM("scon"))
        sBB = Ctr(SEM("sbb"))
        sYB = Ctr(SEM("syb"))
        sXI = Ctr(SEM("sxi"))
        sBCD = Ctr(SEM("sbcd"))
        OUT = Ctr(SEM("outsem"))
        PREW = [Ctr(SEM(f"sprew{i}")) for i in range(5)]

        def slot_sems(name, k):
            return [Ctr(SEM(f"{name}{i}")) for i in range(k)]

        sPW = slot_sems("spw", 2)
        sXA = slot_sems("sxa", 2)
        sDT = slot_sems("sdt", 3)
        sSZ = slot_sems("ssz", 3)
        sFX = slot_sems("sfx", 2)
        sQF = slot_sems("sqf", 4)
        sXI = slot_sems("sxi", 4)
        sXN = slot_sems("sxn", 2)
        sSZS = slot_sems("sszs", 2)
        sDTS = slot_sems("sdts", 4)
        sYIN = slot_sems("syin", 2)
        sX1S = slot_sems("sx1s", 2)
        sIQS = slot_sems("siqs", 2)
        sQ2S = slot_sems("sq2s", 2)
        sWQ = slot_sems("swq", 2)
        sOUTS = slot_sems("souts", 4)
        sGPP = slot_sems("sgpp", 2)
        sGPF = slot_sems("sgpf", 1)
        sGPH = slot_sems("sgph", 1)

        def sl_issue(dma, ss, i):
            c = ss[i % len(ss)]
            dma.then_inc(c.sem, 16)
            return (i % len(ss), c.inc(16))

        def sl_wait(eng, ss, mark):
            if mark:
                eng.wait_ge(ss[mark[0]].sem, mark[1])

        def sl_wait_all(eng, ss):
            for c in ss:
                if c.n:
                    eng.wait_ge(c.sem, c.n)

        _SEQ_ONLY = {"wait_ge", "nop", "sem_inc", "dma_start", "drain",
                     "maybe_wait_ge", "wait_op"}

        def _ap_name(x):
            try:
                return x.tensor.name
            except AttributeError:
                return None

        class _Trk:
            DEPTH = 6

            def __init__(self, eng):
                self._eng = eng
                self.last = None
                self._hist = []

            def __getattr__(self, name):
                f = getattr(self._eng, name)
                if callable(f) and not name.startswith("_"):
                    trk = self

                    def g(*a, _f=f, _name=name, **k):
                        if _name not in _SEQ_ONLY:
                            aps = [x for x in list(a) + list(k.values())
                                   if _ap_name(x) is not None]
                            wr = {_ap_name(aps[0])} if aps else set()
                            for kk in ("out", "accum_out"):
                                if _ap_name(k.get(kk)) is not None:
                                    wr.add(_ap_name(k[kk]))
                            rd = {_ap_name(x) for x in aps} - wr
                            hazard = any((rd & hwr) or (wr & hwr) or (wr & hrd)
                                         for hrd, hwr in trk._hist)
                            if hazard:
                                trk._eng.drain()
                                trk._hist = []
                        r = _f(*a, **k)
                        if _name not in _SEQ_ONLY and hasattr(r, "then_inc"):
                            trk.last = r
                            trk._hist.append((rd, wr))
                            if len(trk._hist) > trk.DEPTH:
                                trk._hist.pop(0)
                        return r
                    return g
                return f

        TEN = _Trk(nc.tensor)
        VEC = _Trk(nc.vector)
        SCA = _Trk(nc.scalar)
        GPS = _Trk(nc.gpsimd)
        SYN = nc.sync

        _marked = {}

        def _mk(trk, ctr, key):
            if trk.last is _marked.get(key):
                return ctr.n
            _marked[key] = trk.last
            trk.last.then_inc(ctr.sem, 1)
            return ctr.inc()

        def pe_mark():
            return _mk(TEN, PE, "pe")

        def sc_mark():
            return _mk(SCA, SC, "sc")

        def dv_mark():
            return _mk(VEC, DV, "dv")

        def gp_mark():
            return _mk(GPS, GP, "gp")

        with nc.Block():
            for i, v in enumerate((EPS, 1.0, 0.5, -0.5, -1.0)):
                VEC.memset(cst[:, i:i + 1], v)
                nc.const_aps.register(F32, v, cst[:, i:i + 1])
            VEC.memset(ones_f[:, :], 1.0)
            VEC.memset(onescol_f[:, :], 1.0)
            VEC.memset(h_state[:, :], 0.0)
            init_dv = dv_mark()
            SCA.wait_ge(DV.sem, init_dv)
            TEN.wait_ge(DV.sem, init_dv)

            for i, (dst, src) in enumerate(((win_s, w_in), (wdt_s, w_dt),
                                            (wo_s, w_out), (wb_s, w_b),
                                            (wc_s, w_c))):
                GPS.dma_start(out=dst[:, :], in_=src[:, :]
                              ).then_inc(PREW[i].sem, 16)
                PREW[i].inc(16)

            with nc.sbuf_tensor("at_l", [128, EC * N], F32) as at_l:
                for c in range(EC):
                    SYN.dma_start(out=at_l[:, c * N:(c + 1) * N],
                                  in_=a_log[c * 128:(c + 1) * 128, :]
                                  ).then_inc(sCON.sem, 16)
                    sCON.inc(16)
                    SYN.dma_start(out=bdt_all[:, c:c + 1],
                                  in_=b_dt[c * 128:(c + 1) * 128]
                                  ).then_inc(sCON.sem, 16)
                    sCON.inc(16)
                    SYN.dma_start(out=D_all[:, c:c + 1],
                                  in_=d_vec[c * 128:(c + 1) * 128]
                                  ).then_inc(sCON.sem, 16)
                    sCON.inc(16)
                SYN.dma_start(out=tiny[0:1, 0:1], in_=gate[:]
                              ).then_inc(sCON.sem, 16)
                sCON.inc(16)

                SCA.wait_ge(sCON.sem, sCON.n)
                SCA.activation(A_all[:, :], at_l[:, :], ACTF.Exp)
                SCA.activation(tiny[0:1, 1:2], tiny[0:1, 0:1], ACTF.Exp,
                               scale=-1.0)
                sc_prep = sc_mark()
                VEC.wait_ge(SC.sem, sc_prep)
                VEC.tensor_scalar_mul(A_all[:, :], A_all[:, :], -1.0)
                VEC.tensor_scalar_add(tiny[0:1, 2:3], tiny[0:1, 1:2], 1.0)
                VEC.reciprocal(tiny[0:1, 3:4], tiny[0:1, 2:3])
                dv_sig = dv_mark()
                TEN.wait_ge(DV.sem, dv_sig)
                TEN.matmul(mid_ps[0][:, 0:1], ones_f[0:1, :],
                           tiny[0:1, 3:4], start=True, stop=True)
                pe_g = pe_mark()
                VEC.wait_ge(PE.sem, pe_g)
                VEC.tensor_copy(cols[:, 0:1], mid_ps[0][:, 0:1])
                dv_mark()

            with (nc.sbuf_tensor("pwb0", [128, II], F32) as pw0,
                  nc.sbuf_tensor("pwb1", [128, II], F32) as pw1,
                  nc.sbuf_tensor("pq0", [128, II], BF16) as pq0,
                  nc.sbuf_tensor("pq1", [128, II], BF16) as pq1,
                  nc.sbuf_tensor("pt0", [128, II], F32) as pt0,
                  nc.sbuf_tensor("pt1", [128, II], F32) as pt1,
                  nc.sbuf_tensor("pacc", [128, 2], F32) as pacc,
                  nc.sbuf_tensor("pscr", [128, II], F32) as pscr,
                  nc.sbuf_tensor("pca", [128, II], F32) as pca,
                  nc.sbuf_tensor("pcb", [128, II], F32) as pcb):
                pwb = [pw0, pw1]
                pqb = [pq0, pq1]
                ptb = [pt0, pt1]
                pli = 0
                qsi = 0
                pl_gate = {}
                st_q = {}
                w_done = {}
                for wsrc, wdst, wint, sci, mci in (
                        (w_g, wg_s, wgi_s, 1, 4), (w_u, wu_s, wui_s, 2, 5),
                        (w_d, wd_s, wdi_s, 3, 6)):
                    cw = II if wsrc is w_d else D
                    nrow = (D if wsrc is w_d else II) // 128
                    dv_acc = {}
                    for r in range(nrow):
                        SYN.wait_ge(SC.sem, pl_gate.get(pli - 2, sc_prep))
                        m = sl_issue(SYN.dma_start(
                            out=pwb[pli % 2][:, 0:cw],
                            in_=wsrc[r * 128:(r + 1) * 128, :]), sPW, pli)
                        SCA.wait_ge(DV.sem, dv_acc.get(r - 1, 0))
                        sl_wait(SCA, sPW, m)
                        SCA.activation(pscr[:, 0:cw], pwb[pli % 2][:, 0:cw],
                                       ACTF.Abs, accum_out=tiny[:, 4:5])
                        pl_gate[pli] = sc_mark()
                        pli += 1
                        VEC.wait_ge(SC.sem, pl_gate[pli - 1])
                        if r == 0:
                            VEC.tensor_copy(pacc[:, 0:1], tiny[:, 4:5])
                        else:
                            VEC.tensor_add(pacc[:, 0:1], pacc[:, 0:1],
                                           tiny[:, 4:5])
                        dv_acc[r] = dv_mark()
                    TEN.wait_ge(DV.sem, dv_acc[nrow - 1])
                    TEN.matmul(mid_ps[0][0:1, 0:1], onescol_f[:, 0:1],
                               pacc[:, 0:1], start=True, stop=True)
                    pe_tot = pe_mark()
                    VEC.wait_ge(PE.sem, pe_tot)
                    VEC.tensor_scalar_mul(tiny[0:1, 5:6], mid_ps[0][0:1, 0:1],
                                          1.0 / (II * D))
                    VEC.tensor_scalar_max(tiny[0:1, 5:6], tiny[0:1, 5:6], 1e-5)
                    VEC.reciprocal(tiny[0:1, 6:7], tiny[0:1, 5:6])
                    dv_sc = dv_mark()
                    TEN.wait_ge(DV.sem, dv_sc)
                    TEN.matmul(mid_ps[0][:, 1:3], ones_f[0:1, :],
                               tiny[0:1, 5:7], start=True, stop=True)
                    pe_bc = pe_mark()
                    VEC.wait_ge(PE.sem, pe_bc)
                    VEC.tensor_copy(cols[:, mci:mci + 1], mid_ps[0][:, 1:2])
                    VEC.tensor_copy(cols[:, sci:sci + 1], mid_ps[0][:, 2:3])
                    dv_cols = dv_mark()
                    dv_q = {}
                    gp_c = {}
                    for r in range(nrow):
                        SYN.wait_ge(SC.sem, pl_gate.get(pli - 2, 0))
                        SYN.wait_ge(DV.sem, dv_cols)
                        m = sl_issue(SYN.dma_start(
                            out=pwb[pli % 2][:, 0:cw],
                            in_=wsrc[r * 128:(r + 1) * 128, :]), sPW, pli)
                        SCA.wait_ge(DV.sem, dv_q.get(r - 2, dv_cols))
                        sl_wait(SCA, sPW, m)
                        SCA.activation(ptb[r % 2][:, 0:cw],
                                       pwb[pli % 2][:, 0:cw], ACTF.Copy,
                                       scale=cols[:, sci:sci + 1])
                        pl_gate[pli] = sc_mark()
                        pli += 1
                        VEC.wait_ge(SC.sem, pl_gate[pli - 1])
                        sl_wait(VEC, sGPP, gp_c.get(r - 2))
                        VEC.tensor_scalar(ptb[r % 2][:, 0:cw],
                                          ptb[r % 2][:, 0:cw], 1.0, -1.0,
                                          ALU.min, ALU.max)
                        dv_t = dv_mark()
                        GPS.wait_ge(DV.sem, dv_t)
                        mo = sl_issue(GPS.dma_start(
                            out=wint[r * 128:(r + 1) * 128, :],
                            in_=ptb[r % 2][:, 0:cw]), sGPP, r)
                        sl_wait(GPS, sGPP, mo)
                        mi = sl_issue(GPS.dma_start(
                            out=pscr[:, 0:cw],
                            in_=wint[r * 128:(r + 1) * 128, :]), sGPP, r)
                        gp_c[r] = mi
                        sl_wait(VEC, sGPP, mi)
                        sl_wait(VEC, sWQ, st_q.get(qsi - 2))
                        VEC.scalar_tensor_tensor(pca[:, 0:cw], pscr[:, 0:cw],
                                                 0.5, ptb[r % 2][:, 0:cw],
                                                 ALU.add, ALU.is_lt)
                        VEC.scalar_tensor_tensor(pcb[:, 0:cw], pscr[:, 0:cw],
                                                 -0.5, ptb[r % 2][:, 0:cw],
                                                 ALU.add, ALU.is_ge)
                        VEC.tensor_sub(pca[:, 0:cw], pca[:, 0:cw], pcb[:, 0:cw])
                        VEC.tensor_add(pqb[qsi % 2][:, 0:cw], pca[:, 0:cw],
                                       pscr[:, 0:cw])
                        dv_q[r] = dv_mark()
                        SYN.wait_ge(DV.sem, dv_q[r])
                        st_q[qsi] = sl_issue(SYN.dma_start(
                            out=wdst[r * 128:(r + 1) * 128, :],
                            in_=pqb[qsi % 2][:, 0:cw]), sWQ, qsi)
                        qsi += 1
                    w_done[wdst is wg_s and "g" or (wdst is wu_s and "u" or "d")] \
                        = dict((i, c.n) for i, c in enumerate(sWQ))
                prep_dv = DV.n
                prep_sc = SC.n
                prep_gp = GP.n

            with (nc.sbuf_tensor("xg0", [128, D], F32) as xg0,
                  nc.sbuf_tensor("xg1", [128, D], F32) as xg1,
                  nc.sbuf_tensor("xng0", [128, D], BF16) as xng0,
                  nc.sbuf_tensor("xng1", [128, D], BF16) as xng1,
                  nc.sbuf_tensor("sq_scr", [128, D], F32) as sq_scr,
                  nc.sbuf_tensor("lncol", [128, 4], F32) as lncol):
                xgs = [xg0, xg1]
                xngs = [xng0, xng1]
                SYN.wait_ge(DV.sem, prep_dv)
                SYN.wait_ge(SC.sem, prep_sc)
                sl_wait_all(SYN, sWQ)
                sl_wait_all(SYN, sGPP)
                xn_dv = {}
                xn_st = {}
                for tg in range(NG):
                    SYN.wait_ge(DV.sem, xn_dv.get(tg - 2, prep_dv))
                    mx = sl_issue(SYN.dma_start(
                        out=xgs[tg % 2][:, :],
                        in_=x_in[tg * TG:(tg + 1) * TG, :]), sXA, tg)
                    xgt, xnt = xgs[tg % 2], xngs[tg % 2]
                    SCA.wait_ge(DV.sem, xn_dv.get(tg - 1, prep_dv))
                    sl_wait(SCA, sXA, mx)
                    SCA.activation(sq_scr[:, :], xgt[:, :], ACTF.Square,
                                   accum_out=lncol[:, 1:2])
                    sc_sq = sc_mark()
                    sl_wait(VEC, sXA, mx)
                    sl_wait(VEC, sXN, xn_st.get(tg - 2))
                    VEC.reduce_sum(lncol[:, 0:1], xgt[:, :], AXX)
                    VEC.tensor_scalar_mul(lncol[:, 0:1], lncol[:, 0:1], 1.0 / D)
                    VEC.tensor_mul(lncol[:, 2:3], lncol[:, 0:1], lncol[:, 0:1])
                    VEC.wait_ge(SC.sem, sc_sq)
                    VEC.scalar_tensor_tensor(lncol[:, 3:4], lncol[:, 1:2],
                                             1.0 / D, lncol[:, 2:3],
                                             ALU.mult, ALU.subtract)
                    dv_var = dv_mark()
                    SCA.wait_ge(DV.sem, dv_var)
                    SCA.activation(lncol[:, 2:3], lncol[:, 3:4], ACTF.Ln,
                                   bias=EPS)
                    SCA.activation(lncol[:, 3:4], lncol[:, 2:3], ACTF.Exp,
                                   scale=-0.5)
                    sc_rs = sc_mark()
                    VEC.wait_ge(SC.sem, sc_rs)
                    VEC.tensor_scalar(xnt[:, :], xgt[:, :], lncol[:, 0:1],
                                      lncol[:, 3:4], ALU.subtract, ALU.mult)
                    xn_dv[tg] = dv_mark()
                    SYN.wait_ge(DV.sem, xn_dv[tg])
                    xn_st[tg] = sl_issue(SYN.dma_start(
                        out=xn_s[tg * TG:(tg + 1) * TG, :],
                        in_=xnt[:, :]), sXN, tg)

            a_dv = DV.n
            a_sc = SC.n

            sl_wait_all(SYN, sXN)
            SYN.wait_ge(DV.sem, a_dv)
            SYN.wait_ge(SC.sem, a_sc)
            for dc in range(DC):
                SYN.dma_start(out=xnF[:, dc * L:(dc + 1) * L],
                              in_=xn_s[:, dc * 128:(dc + 1) * 128],
                              transpose=True).then_inc(sXF.sem, 16)
                sXF.inc(16)
            xnF_ld = sXF.n

            with nc.sbuf_tensor("xssmF", [128, EC * L], BF16) as xssmF:

                with (nc.sbuf_tensor("wbuf", [128, 32 * 512], BF16,
                                     is_transpose_dest=True) as wbuf,
                      nc.sbuf_tensor("szev0", [128, 512], BF16) as szev0,
                      nc.sbuf_tensor("szev1", [128, 512], BF16) as szev1,
                      nc.sbuf_tensor("wbc", [128, EC * 32], BF16,
                                     is_transpose_dest=True) as wbc,
                      nc.sbuf_tensor("dtev", [128, 8 * 512], F32) as dtev):
                    szevs = [szev0, szev1]
                    wslot = lambda k: wbuf[:, k * 512:(k + 1) * 512]

                    mg_pe_done = {}
                    ev_sc = {}
                    evg = 0
                    szev_free = {}
                    szi = 0
                    for mg in range(8):
                        for k in range(DC):
                            SYN.wait_ge(PREW[0].sem, 16)
                            if mg > 1:
                                SYN.wait_ge(PE.sem, mg_pe_done[mg - 2])
                            SYN.dma_start(
                                out=wslot((mg % 2) * 8 + k),
                                in_=win_s[mg * 512:(mg + 1) * 512,
                                          k * 128:(k + 1) * 128],
                                transpose=True).then_inc(WSTp[mg % 2].sem, 16)
                            WSTp[mg % 2].inc(16)
                        blk_rd = WSTp[mg % 2].n
                        for fc in range(4):
                            grp = big_ps if evg % 2 == 0 else mid_ps
                            TEN.wait_ge(WSTp[mg % 2].sem, blk_rd)
                            if mg == 0 and fc == 0:
                                TEN.wait_ge(sXF.sem, xnF_ld)
                                TEN.wait_ge(DV.sem, prep_dv)
                            TEN.wait_ge(SC.sem, ev_sc.get(evg - 2, 0))
                            for k in range(DC):
                                for m in range(4):
                                    TEN.matmul(
                                        grp[m][:, :],
                                        wslot((mg % 2) * 8 + k)[:,
                                            m * 128:(m + 1) * 128],
                                        xnF[:, k * L + fc * 512:
                                            k * L + fc * 512 + 512],
                                        start=(k == 0), stop=(k == DC - 1))
                            pe_mm = pe_mark()
                            SCA.wait_ge(PE.sem, pe_mm)
                            for m in range(4):
                                mt = mg * 4 + m
                                if mt < EC:
                                    SCA.copy(
                                        xssmF[:, mt * L + fc * 512:
                                              mt * L + fc * 512 + 512],
                                        grp[m][:, :])
                                else:
                                    zi = mt - EC
                                    slot = szi % 2
                                    sl = szevs[slot]
                                    sl_wait(SCA, sSZS, szev_free.get(slot))
                                    SCA.activation(sl[:, :], grp[m][:, :],
                                                   ACTF.Silu)
                                    scm = sc_mark()
                                    SYN.wait_ge(SC.sem, scm)
                                    szev_free[slot] = sl_issue(SYN.dma_start(
                                        out=sz_s[zi, :, fc * 512:fc * 512 + 512],
                                        in_=sl[:, :]), sSZS, szi)
                                    szi += 1
                            ev_sc[evg] = sc_mark()
                            evg += 1
                        mg_pe_done[mg] = PE.n
                    c_done_sc = SC.n

                    dtev_free = {}
                    dti = 0
                    for mg in range(4):
                        for k in range(EC):
                            SYN.wait_ge(PREW[1].sem, 16)
                            if mg == 0 and k == 0:
                                SYN.wait_ge(PE.sem, mg_pe_done[6])
                            elif mg == 1 and k == 0:
                                SYN.wait_ge(PE.sem, mg_pe_done[7])
                            elif k == 0:
                                SYN.wait_ge(PE.sem, mg_pe_done[("d", mg - 2)])
                            SYN.dma_start(
                                out=wslot((mg % 2) * 16 + k),
                                in_=wdt_s[mg * 512:(mg + 1) * 512,
                                          k * 128:(k + 1) * 128],
                                transpose=True).then_inc(WSTp[mg % 2].sem, 16)
                            WSTp[mg % 2].inc(16)
                        blk_rd = WSTp[mg % 2].n
                        for fc in range(4):
                            grp = big_ps if evg % 2 == 0 else mid_ps
                            TEN.wait_ge(WSTp[mg % 2].sem, blk_rd)
                            TEN.wait_ge(SC.sem, ev_sc.get(evg - 2, 0))
                            for k in range(EC):
                                for m in range(4):
                                    TEN.matmul(
                                        grp[m][:, :],
                                        wslot((mg % 2) * 16 + k)[:,
                                            m * 128:(m + 1) * 128],
                                        xssmF[:, k * L + fc * 512:
                                              k * L + fc * 512 + 512],
                                        start=(k == 0), stop=(k == EC - 1))
                            pe_mm = pe_mark()
                            SCA.wait_ge(PE.sem, pe_mm)
                            for m in range(4):
                                ec = mg * 4 + m
                                sl_wait(SCA, sDTS, dtev_free.get(m))
                                e_sl = dtev[:, m * 512:(m + 1) * 512]
                                d_sl = dtev[:, (4 + m) * 512:(5 + m) * 512]
                                SCA.activation(e_sl, grp[m][:, :], ACTF.Exp,
                                               bias=bdt_all[:, ec:ec + 1])
                                SCA.activation(d_sl, e_sl, ACTF.Ln, bias=1.0)
                                scm = sc_mark()
                                SYN.wait_ge(SC.sem, scm)
                                dtev_free[m] = sl_issue(SYN.dma_start(
                                    out=dt_s[ec, :, fc * 512:fc * 512 + 512],
                                    in_=d_sl), sDTS, dti)
                                dti += 1
                            ev_sc[evg] = sc_mark()
                            evg += 1
                        mg_pe_done[("d", mg)] = PE.n

                    for k in range(EC):
                        SYN.wait_ge(PREW[3].sem, 16)
                        SYN.wait_ge(PREW[4].sem, 16)
                        SYN.dma_start(out=wbc[:, k * 32:k * 32 + 16],
                                      in_=wb_s[:, k * 128:(k + 1) * 128],
                                      transpose=True).then_inc(sBC.sem, 16)
                        sBC.inc(16)
                        SYN.dma_start(out=wbc[:, k * 32 + 16:k * 32 + 32],
                                      in_=wc_s[:, k * 128:(k + 1) * 128],
                                      transpose=True).then_inc(sBC.sem, 16)
                        sBC.inc(16)
                    bc_rd = sBC.n
                    bc_ev = {}
                    for fc in range(4):
                        TEN.wait_ge(sBC.sem, bc_rd)
                        TEN.wait_ge(SC.sem, bc_ev.get(fc - 1, ev_sc[evg - 1]))
                        for k in range(EC):
                            TEN.matmul(
                                mid_ps[0][0:32, :],
                                wbc[:, k * 32:k * 32 + 32],
                                xssmF[:, k * L + fc * 512:k * L + fc * 512 + 512],
                                start=(k == 0), stop=(k == EC - 1))
                        pe_bc2 = pe_mark()
                        SCA.wait_ge(PE.sem, pe_bc2)
                        SCA.copy(BC_sb[0:32, fc * 512:fc * 512 + 512],
                                 mid_ps[0][0:32, :])
                        bc_ev[fc] = sc_mark()
                    SYN.wait_ge(SC.sem, bc_ev[3])
                    SYN.dma_start(out=bc_d[:, :], in_=BC_sb[0:32, :]
                                  ).then_inc(sBCD.sem, 16)
                    sBCD.inc(16)
                    d_done_sc = SC.n

                with (nc.sbuf_tensor("dtb", [128, 3 * 512], F32) as dtb,
                      nc.sbuf_tensor("szb", [128, 3 * 512], BF16) as szb,
                      nc.sbuf_tensor("wbs", [128, 2 * 512], F32) as wbs,
                      nc.sbuf_tensor("abz", [128, 2 * 512], F32) as abz,
                      nc.sbuf_tensor("ubz", [128, 2 * 512], F32) as ubz,
                      nc.sbuf_tensor("Hbz", [128, 2 * 512], F32) as Hbz,
                      nc.sbuf_tensor("hcz", [128, 2 * 512], F32) as hcz,
                      nc.sbuf_tensor("yccz", [128, 2 * 512], F32) as yccz,
                      nc.sbuf_tensor("Bb", [128, N * 512], BF16) as Bb,
                      nc.sbuf_tensor("Cb", [128, N * 512], BF16) as Cb,
                      nc.sbuf_tensor("yinsl", [128, 2 * 512], BF16) as yinsl):
                    sl = lambda t, i, w=512: t[:, i * w:(i + 1) * w]
                    sl_wait_all(SYN, sDTS)
                    sl_wait_all(SYN, sSZS)
                    dt_ld = {}
                    sz_ld = {}
                    dt_dv_free = {}
                    sc_a = {}
                    ab_free = {}
                    hc_rdy = {}
                    gp_n = {}
                    y_fold = {}
                    yin_st = {}
                    bb_ld = {}
                    for ch in range(NCH):
                        prev_scan_dv = y_fold.get(ch * EC - 1, 0)
                        SYN.wait_ge(sBCD.sem, sBCD.n)
                        SYN.wait_ge(DV.sem, prev_scan_dv)
                        for n in range(N):
                            for srow, dst in ((n, Bb), (N + n, Cb)):
                                SYN.dma_start(
                                    out=sl(dst, n),
                                    in_=bc_d[srow:srow + 1,
                                             ch * 512:(ch + 1) * 512]
                                    .partition_broadcast(128)
                                    ).then_inc(sBB.sem, 16)
                                sBB.inc(16)
                        bb_ld[ch] = sBB.n

                        for c in range(EC):
                            ci = ch * EC + c
                            s3 = ci % 3
                            s2 = ci % 2
                            SYN.wait_ge(DV.sem, dt_dv_free.get(ci - 3, 0))
                            SYN.wait_ge(SC.sem, sc_a.get(ci - 3, 0))
                            dt_ld[ci] = sl_issue(SYN.dma_start(
                                out=sl(dtb, s3),
                                in_=dt_s[c, :, ch * 512:ch * 512 + 512]),
                                sDT, ci)
                            sz_ld[ci] = sl_issue(SYN.dma_start(
                                out=sl(szb, s3),
                                in_=sz_s[c, :, ch * 512:ch * 512 + 512]),
                                sSZ, ci)
                            sl_wait(VEC, sDT, dt_ld[ci])
                            if c == 0:
                                VEC.wait_ge(sBB.sem, bb_ld[ch])
                            VEC.tensor_mul(
                                sl(wbs, s2), sl(dtb, s3),
                                xssmF[:, c * L + ch * 512:c * L + ch * 512 + 512])
                            w_rdy = dv_mark()
                            for n in range(N):
                                cn = c * N + n
                                p2 = n % 2
                                sl_wait(SCA, sDT, dt_ld[ci])
                                SCA.wait_ge(DV.sem,
                                            ab_free.get(p2, 0))
                                SCA.activation(sl(abz, p2), sl(dtb, s3),
                                               ACTF.Exp,
                                               scale=A_all[:, cn:cn + 1])
                                sc_an = sc_mark()
                                VEC.wait_ge(SC.sem, sc_an)
                                VEC.wait_ge(GP.sem,
                                            gp_n.get((ci, n - 2),
                                                     gp_n.get((ci - 1,
                                                               N - 2 + n), 0)))
                                VEC.tensor_mul(sl(ubz, p2), sl(wbs, s2),
                                               sl(Bb, n))
                                VEC.tensor_tensor_scan(
                                    sl(Hbz, p2), sl(abz, p2), sl(ubz, p2),
                                    h_state[:, cn:cn + 1], ALU.mult, ALU.add)
                                VEC.tensor_mul(sl(hcz, p2), sl(Hbz, p2),
                                               sl(Cb, n))
                                hc_rdy[(ci, n)] = dv_mark()
                                ab_free[p2] = hc_rdy[(ci, n)]
                                GPS.wait_ge(DV.sem, hc_rdy[(ci, n)])
                                if n == 0:
                                    GPS.wait_ge(DV.sem, y_fold.get(ci - 2, 0))
                                    GPS.tensor_copy(sl(yccz, ci % 2),
                                                    sl(hcz, p2))
                                else:
                                    GPS.tensor_add(sl(yccz, ci % 2),
                                                   sl(yccz, ci % 2),
                                                   sl(hcz, p2))
                                GPS.tensor_copy(h_state[:, cn:cn + 1],
                                                sl(Hbz, p2)[:, 511:512])
                                gp_n[(ci, n)] = gp_mark()
                            sc_a[ci] = SC.n
                            VEC.wait_ge(GP.sem, gp_n[(ci, N - 1)])
                            sl_wait(VEC, sSZ, sz_ld[ci])
                            sl_wait(VEC, sYIN, yin_st.get(ci - 2))
                            VEC.scalar_tensor_tensor(
                                sl(wbs, s2),
                                xssmF[:, c * L + ch * 512:c * L + ch * 512 + 512],
                                D_all[:, c:c + 1], sl(yccz, ci % 2),
                                ALU.mult, ALU.add)
                            VEC.scalar_tensor_tensor(
                                sl(yinsl, ci % 2), sl(wbs, s2), 1.0,
                                sl(szb, s3), ALU.mult, ALU.mult)
                            y_fold[ci] = dv_mark()
                            dt_dv_free[ci] = y_fold[ci]
                            SYN.wait_ge(DV.sem, y_fold[ci])
                            yin_st[ci] = sl_issue(SYN.dma_start(
                                out=yin_s[c, :, ch * 512:ch * 512 + 512],
                                in_=sl(yinsl, ci % 2)), sYIN, ci)
                    e_done_sc = SC.n
                    e_done_dv = DV.n
                    e_done_gp = GP.n

                with (nc.sbuf_tensor("wo_blk", [128, EC * 1024], BF16,
                                     is_transpose_dest=True) as wo_blk,
                      nc.sbuf_tensor("yb", [128, EC * 512], BF16) as yb,
                      nc.sbuf_tensor("fxg0", [128, D], F32) as fxg0,
                      nc.sbuf_tensor("fxg1", [128, D], F32) as fxg1,
                      nc.sbuf_tensor("fx10", [128, D], F32) as fx10,
                      nc.sbuf_tensor("fx11", [128, D], F32) as fx11,
                      nc.sbuf_tensor("fiq0", [128, D], BF16) as fiq0,
                      nc.sbuf_tensor("fiq1", [128, D], BF16) as fiq1,
                      nc.sbuf_tensor("qta", [128, D], F32) as qta,
                      nc.sbuf_tensor("qtb", [128, D], F32) as qtb,
                      nc.sbuf_tensor("qtc", [128, D], F32) as qtc,
                      nc.sbuf_tensor("sq2", [128, D], F32) as sq2,
                      nc.sbuf_tensor("lnc2", [128, 4], F32) as lnc2):
                    fxg = [fxg0, fxg1]
                    fx1 = [fx10, fx11]
                    fiq = [fiq0, fiq1]
                    SYN.wait_ge(DV.sem, e_done_dv)
                    SYN.wait_ge(SC.sem, e_done_sc)
                    SYN.wait_ge(GP.sem, e_done_gp)
                    sl_wait_all(SYN, sYIN)
                    sl_wait_all(SYN, sDT)
                    sl_wait_all(SYN, sSZ)
                    for ec in range(EC):
                        for o in range(2):
                            SYN.wait_ge(PREW[2].sem, 16)
                            SYN.dma_start(
                                out=wo_blk[:, (ec * 2 + o) * 512:
                                           (ec * 2 + o) * 512 + 512],
                                in_=wo_s[o * 512:(o + 1) * 512,
                                         ec * 128:(ec + 1) * 128],
                                transpose=True).then_inc(WST.sem, 16)
                            WST.inc(16)
                    wo_rd = WST.n
                    dv_x1m = {}
                    sc_sq2m = {}
                    gp_iq = {}
                    iq_st = {}
                    x1_st = {}
                    fev = {}
                    pe_tq = {}
                    for tq in range(NCH):
                        sl_wait_all(SYN, sYIN)
                        if tq > 0:
                            SYN.wait_ge(PE.sem, pe_tq[tq - 1])
                        for ec in range(EC):
                            SYN.dma_start(
                                out=yb[:, ec * 512:(ec + 1) * 512],
                                in_=yin_s[ec, :, tq * 512:(tq + 1) * 512]
                                ).then_inc(sYB.sem, 16)
                            sYB.inc(16)
                        yb_ld = sYB.n
                        for th in range(2):
                            TEN.wait_ge(WST.sem, wo_rd)
                            TEN.wait_ge(sYB.sem, yb_ld)
                            TEN.wait_ge(SC.sem, e_done_sc)
                            TEN.wait_ge(DV.sem,
                                        fev.get((tq, th - 1),
                                                fev.get((tq - 1, 1), 0)))
                            for ec in range(EC):
                                for t2 in range(2):
                                    tt = th * 2 + t2
                                    for o in range(2):
                                        TEN.matmul(
                                            mid_ps[t2 * 2 + o][:, :],
                                            yb[:, ec * 512 + tt * 128:
                                               ec * 512 + tt * 128 + 128],
                                            wo_blk[:, (ec * 2 + o) * 512:
                                                   (ec * 2 + o) * 512 + 512],
                                            start=(ec == 0),
                                            stop=(ec == EC - 1))
                            pe_o = pe_mark()
                            pe_tq[tq] = pe_o
                            for t2 in range(2):
                                tg = tq * 4 + th * 2 + t2
                                par = tg % 2
                                SYN.wait_ge(DV.sem, dv_x1m.get(tg - 2, 0))
                                mfx = sl_issue(SYN.dma_start(
                                    out=fxg[par][:, :],
                                    in_=x_in[tg * TG:(tg + 1) * TG, :]),
                                    sFX, tg)
                                VEC.wait_ge(PE.sem, pe_o)
                                sl_wait(VEC, sFX, mfx)
                                sl_wait(VEC, sX1S, x1_st.get(tg - 2))
                                VEC.wait_ge(SC.sem, sc_sq2m.get(tg - 2, 0))
                                for o in range(2):
                                    VEC.scalar_tensor_tensor(
                                        fx1[par][:, o * 512:o * 512 + 512],
                                        mid_ps[t2 * 2 + o][:, :], cols[:, 0:1],
                                        fxg[par][:, o * 512:o * 512 + 512],
                                        ALU.mult, ALU.add)
                                dv_x1 = dv_mark()
                                dv_x1m[tg] = dv_x1
                                SYN.wait_ge(DV.sem, dv_x1)
                                x1_st[tg] = sl_issue(SYN.dma_start(
                                    out=x1_s[tg * TG:(tg + 1) * TG, :],
                                    in_=fx1[par][:, :]), sX1S, tg)
                                SCA.wait_ge(DV.sem, dv_x1)
                                SCA.activation(sq2[:, :], fx1[par][:, :],
                                               ACTF.Square,
                                               accum_out=lnc2[:, 1:2])
                                sc_sq2 = sc_mark()
                                sc_sq2m[tg] = sc_sq2
                                VEC.reduce_sum(lnc2[:, 0:1], fx1[par][:, :],
                                               AXX)
                                VEC.tensor_scalar_mul(lnc2[:, 0:1],
                                                      lnc2[:, 0:1], 1.0 / D)
                                VEC.tensor_mul(lnc2[:, 2:3], lnc2[:, 0:1],
                                               lnc2[:, 0:1])
                                VEC.wait_ge(SC.sem, sc_sq2)
                                VEC.scalar_tensor_tensor(
                                    lnc2[:, 3:4], lnc2[:, 1:2], 1.0 / D,
                                    lnc2[:, 2:3], ALU.mult, ALU.subtract)
                                dv_v2 = dv_mark()
                                SCA.wait_ge(DV.sem, dv_v2)
                                SCA.activation(lnc2[:, 2:3], lnc2[:, 3:4],
                                               ACTF.Ln, bias=EPS)
                                SCA.activation(lnc2[:, 3:4], lnc2[:, 2:3],
                                               ACTF.Exp, scale=-0.5)
                                sc_r2 = sc_mark()
                                VEC.wait_ge(SC.sem, sc_r2)
                                sl_wait(VEC, sIQS, iq_st.get(tg - 2))
                                sl_wait(VEC, sGPF, gp_iq.get(tg - 1))
                                VEC.tensor_scalar(qta[:, :], fx1[par][:, :],
                                                  lnc2[:, 0:1], lnc2[:, 3:4],
                                                  ALU.subtract, ALU.mult)
                                VEC.tensor_reduce(lnc2[:, 0:1], qta[:, :], AXX,
                                                  ALU.max,
                                                  apply_absolute_value=True)
                                VEC.tensor_scalar_max(lnc2[:, 0:1],
                                                      lnc2[:, 0:1], 1e-5)
                                VEC.tensor_scalar_mul(stinv[:, tg:tg + 1],
                                                      lnc2[:, 0:1], 1.0 / 127.0)
                                VEC.reciprocal(lnc2[:, 1:2], lnc2[:, 0:1])
                                VEC.tensor_scalar_mul(lnc2[:, 1:2],
                                                      lnc2[:, 1:2], 127.0)
                                VEC.tensor_scalar(qtb[:, :], qta[:, :],
                                                  lnc2[:, 1:2], None, ALU.mult)
                                dv_tq = dv_mark()
                                GPS.wait_ge(DV.sem, dv_tq)
                                mgo = sl_issue(GPS.dma_start(
                                    out=iqi_s[tg * TG:(tg + 1) * TG, :],
                                    in_=qtb[:, :]), sGPF, 0)
                                sl_wait(GPS, sGPF, mgo)
                                mgi = sl_issue(GPS.dma_start(
                                    out=qta[:, :],
                                    in_=iqi_s[tg * TG:(tg + 1) * TG, :]),
                                    sGPF, 0)
                                gp_iq[tg] = mgi
                                sl_wait(VEC, sGPF, mgi)
                                VEC.scalar_tensor_tensor(
                                    sq2[:, :], qta[:, :], 0.5, qtb[:, :],
                                    ALU.add, ALU.is_lt)
                                VEC.scalar_tensor_tensor(
                                    qtc[:, :], qta[:, :], -0.5, qtb[:, :],
                                    ALU.add, ALU.is_ge)
                                VEC.tensor_sub(sq2[:, :], sq2[:, :], qtc[:, :])
                                VEC.tensor_add(fiq[par][:, :], sq2[:, :],
                                               qta[:, :])
                                VEC.tensor_mul(sgam[:, tg:tg + 1],
                                               stinv[:, tg:tg + 1],
                                               cols[:, 4:5])
                                VEC.tensor_mul(suam[:, tg:tg + 1],
                                               stinv[:, tg:tg + 1],
                                               cols[:, 5:6])
                                dv_iq = dv_mark()
                                SYN.wait_ge(DV.sem, dv_iq)
                                iq_st[tg] = sl_issue(SYN.dma_start(
                                    out=iq_s[tg * TG:(tg + 1) * TG, :],
                                    in_=fiq[par][:, :]), sIQS, tg)
                            fev[(tq, th)] = DV.n
                    f_done_dv = DV.n
                    f_done_pe = PE.n
                    f_done_sc = SC.n

            sl_wait_all(SYN, sIQS)
            sl_wait_all(SYN, sX1S)
            sl_wait_all(SYN, sFX)
            sl_wait_all(SYN, sGPF)
            SYN.wait_ge(PE.sem, f_done_pe)
            SYN.wait_ge(DV.sem, f_done_dv)
            SYN.wait_ge(SC.sem, f_done_sc)
            for dc in range(DC):
                SYN.dma_start(out=xnF[:, dc * L:(dc + 1) * L],
                              in_=iq_s[:, dc * 128:(dc + 1) * 128],
                              transpose=True).then_inc(sXF.sem, 16)
                sXF.inc(16)
            iqF_ld = sXF.n

            with nc.sbuf_tensor("wgb", [128, 8 * II], BF16,
                                is_transpose_dest=True) as wgb:
                with (nc.sbuf_tensor("sgb0", [128, II], F32) as sgb0,
                      nc.sbuf_tensor("sgb1", [128, II], F32) as sgb1,
                      nc.sbuf_tensor("prod0", [128, II], F32) as prod0,
                      nc.sbuf_tensor("prod1", [128, II], F32) as prod1,
                      nc.sbuf_tensor("u2b0", [128, 512], F32) as u2b0,
                      nc.sbuf_tensor("u2b1", [128, 512], F32) as u2b1,
                      nc.sbuf_tensor("qva", [128, 512], F32) as qva,
                      nc.sbuf_tensor("qvb", [128, 512], F32) as qvb,
                      nc.sbuf_tensor("qtt", [128, II], F32) as qtt,
                      nc.sbuf_tensor("q2sl", [128, 2 * 512], BF16) as q2sl,
                      nc.sbuf_tensor("qcl", [128, 4], F32) as qcl):
                    sgb = [sgb0, sgb1]
                    prodb = [prod0, prod1]
                    u2b = [u2b0, u2b1]
                    pe_hblk = {}
                    q2_st = {}
                    gp_q2 = {}
                    q2dv = {}
                    prod_dv = {}
                    q2i = 0
                    for hc in range(8):
                        for k in range(DC):
                            if hc == 0 and k == 0:
                                sl_wait_all(SYN, sWQ)
                                SYN.wait_ge(DV.sem, f_done_dv)
                            if k == 0 and hc > 0:
                                SYN.wait_ge(PE.sem, pe_hblk[("u", hc - 1)])
                            SYN.dma_start(out=wgb[:, k * II:(k + 1) * II],
                                          in_=wg_s[:, k * 128:(k + 1) * 128],
                                          transpose=True).then_inc(WST.sem, 16)
                            WST.inc(16)
                        wg_rd = WST.n
                        sgev = {}
                        for o in range(8):
                            for tg2 in range(2):
                                tg = hc * 2 + tg2
                                bank = mid_ps[2 * (o % 2) + tg2]
                                TEN.wait_ge(WST.sem, wg_rd)
                                if hc == 0 and o == 0 and tg2 == 0:
                                    TEN.wait_ge(sXF.sem, iqF_ld)
                                TEN.wait_ge(SC.sem, sgev.get((o - 2, tg2), 0))
                                for k in range(DC):
                                    TEN.matmul(
                                        bank[:, :],
                                        xnF[:, k * L + tg * 128:
                                            k * L + tg * 128 + 128],
                                        wgb[:, k * II + o * 512:
                                            k * II + o * 512 + 512],
                                        start=(k == 0), stop=(k == DC - 1))
                                pe_g2 = pe_mark()
                                SCA.wait_ge(PE.sem, pe_g2)
                                SCA.wait_ge(DV.sem, q2dv.get((hc - 1, tg2), 0))
                                SCA.activation(
                                    sgb[tg2][:, o * 512:o * 512 + 512],
                                    bank[:, :], ACTF.Silu,
                                    scale=sgam[:, tg:tg + 1])
                                sgev[(o, tg2)] = sc_mark()
                        pe_hblk[("g", hc)] = PE.n
                        for k in range(DC):
                            if k == 0:
                                SYN.wait_ge(PE.sem, pe_hblk[("g", hc)])
                            SYN.dma_start(out=wgb[:, k * II:(k + 1) * II],
                                          in_=wu_s[:, k * 128:(k + 1) * 128],
                                          transpose=True).then_inc(WST.sem, 16)
                            WST.inc(16)
                        wu_rd = WST.n
                        uev = {}
                        for o in range(8):
                            for tg2 in range(2):
                                tg = hc * 2 + tg2
                                bank = mid_ps[2 * (o % 2) + tg2]
                                TEN.wait_ge(WST.sem, wu_rd)
                                TEN.wait_ge(SC.sem,
                                            uev.get((o - 2, tg2),
                                                    sgev[(6 + (o % 2), tg2)]))
                                for k in range(DC):
                                    TEN.matmul(
                                        bank[:, :],
                                        xnF[:, k * L + tg * 128:
                                            k * L + tg * 128 + 128],
                                        wgb[:, k * II + o * 512:
                                            k * II + o * 512 + 512],
                                        start=(k == 0), stop=(k == DC - 1))
                                pe_u2 = pe_mark()
                                par = tg2
                                SCA.wait_ge(PE.sem, pe_u2)
                                SCA.wait_ge(DV.sem,
                                            prod_dv.get((hc, o - 1, tg2), 0))
                                SCA.activation(u2b[par][:, :],
                                               bank[:, :],
                                               ACTF.Copy,
                                               scale=suam[:, tg:tg + 1])
                                uev[(o, tg2)] = sc_mark()
                                VEC.wait_ge(SC.sem, uev[(o, tg2)])
                                VEC.tensor_mul(
                                    prodb[tg2][:, o * 512:o * 512 + 512],
                                    sgb[tg2][:, o * 512:o * 512 + 512],
                                    u2b[par][:, :])
                                prod_dv[(hc, o, tg2)] = dv_mark()
                        pe_hblk[("u", hc)] = PE.n
                        for tg2 in range(2):
                            tg = hc * 2 + tg2
                            VEC.wait_ge(DV.sem, prod_dv[(hc, 7, tg2)])
                            sl_wait(VEC, sGPH, gp_q2.get(tg - 1))
                            VEC.tensor_reduce(qcl[:, 0:1], prodb[tg2][:, :],
                                              AXX, ALU.max,
                                              apply_absolute_value=True)
                            VEC.tensor_scalar_max(qcl[:, 0:1], qcl[:, 0:1],
                                                  1e-5)
                            VEC.tensor_scalar_mul(qcl[:, 1:2], qcl[:, 0:1],
                                                  1.0 / 127.0)
                            VEC.tensor_mul(beta[:, tg:tg + 1], qcl[:, 1:2],
                                           cols[:, 6:7])
                            VEC.reciprocal(qcl[:, 2:3], qcl[:, 0:1])
                            VEC.tensor_scalar_mul(qcl[:, 2:3], qcl[:, 2:3],
                                                  127.0)
                            VEC.tensor_scalar(qtt[:, :], prodb[tg2][:, :],
                                              qcl[:, 2:3], None, ALU.mult)
                            dv_tq2 = dv_mark()
                            GPS.wait_ge(DV.sem, dv_tq2)
                            mgo = sl_issue(GPS.dma_start(
                                out=q2i_s[tg * TG:(tg + 1) * TG, :],
                                in_=qtt[:, :]), sGPH, 0)
                            sl_wait(GPS, sGPH, mgo)
                            mgi = sl_issue(GPS.dma_start(
                                out=sgb[tg2][:, :],
                                in_=q2i_s[tg * TG:(tg + 1) * TG, :]),
                                sGPH, 0)
                            gp_q2[tg] = mgi
                            sl_wait(VEC, sGPH, mgi)
                            for o in range(8):
                                r0 = sgb[tg2][:, o * 512:o * 512 + 512]
                                tt_ = qtt[:, o * 512:o * 512 + 512]
                                sl_wait(VEC, sQ2S, q2_st.get(q2i - 2))
                                VEC.scalar_tensor_tensor(
                                    qva[:, :], r0, 0.5, tt_, ALU.add, ALU.is_lt)
                                VEC.scalar_tensor_tensor(
                                    qvb[:, :], r0, -0.5, tt_, ALU.add,
                                    ALU.is_ge)
                                VEC.tensor_sub(qva[:, :], qva[:, :], qvb[:, :])
                                VEC.tensor_add(
                                    q2sl[:, (q2i % 2) * 512:
                                         (q2i % 2) * 512 + 512],
                                    qva[:, :], r0)
                                dv_q2 = dv_mark()
                                SYN.wait_ge(DV.sem, dv_q2)
                                q2_st[q2i] = sl_issue(SYN.dma_start(
                                    out=q2_s[tg * TG:(tg + 1) * TG,
                                             o * 512:o * 512 + 512],
                                    in_=q2sl[:, (q2i % 2) * 512:
                                             (q2i % 2) * 512 + 512]),
                                    sQ2S, q2i)
                                q2i += 1
                            q2dv[(hc, tg2)] = DV.n
                    h_done_pe = PE.n
                    h_done_sc = SC.n
                    h_done_dv = DV.n

                with (nc.sbuf_tensor("q2F0", [128, 512], BF16,
                                     is_transpose_dest=True) as q2F0,
                      nc.sbuf_tensor("q2F1", [128, 512], BF16,
                                     is_transpose_dest=True) as q2F1,
                      nc.sbuf_tensor("q2F2", [128, 512], BF16,
                                     is_transpose_dest=True) as q2F2,
                      nc.sbuf_tensor("q2F3", [128, 512], BF16,
                                     is_transpose_dest=True) as q2F3,
                      nc.sbuf_tensor("ob0", [128, D], F32) as ob0,
                      nc.sbuf_tensor("ob1", [128, D], F32) as ob1,
                      nc.sbuf_tensor("ob2", [128, D], F32) as ob2,
                      nc.sbuf_tensor("ob3", [128, D], F32) as ob3,
                      nc.sbuf_tensor("ix0", [128, D], F32) as ix0,
                      nc.sbuf_tensor("ix1_", [128, D], F32) as ix1_,
                      nc.sbuf_tensor("ix2", [128, D], F32) as ix2,
                      nc.sbuf_tensor("ix3", [128, D], F32) as ix3):
                    q2F = [q2F0, q2F1, q2F2, q2F3]
                    outbs = [ob0, ob1, ob2, ob3]
                    ix1 = [ix0, ix1_, ix2, ix3]
                    SYN.wait_ge(PE.sem, h_done_pe)
                    SYN.wait_ge(SC.sem, h_done_sc)
                    SYN.wait_ge(DV.sem, h_done_dv)
                    sl_wait_all(SYN, sQ2S)
                    sl_wait_all(SYN, sGPH)
                    sl_wait_all(SYN, sX1S)
                    for k in range(32):
                        SYN.dma_start(out=wgb[:, k * 1024:(k + 1) * 1024],
                                      in_=wd_s[:, k * 128:(k + 1) * 128],
                                      transpose=True).then_inc(WST.sem, 16)
                        WST.inc(16)
                    wd_rd = WST.n
                    q2f_free = {}
                    q2fi = 0
                    iev = {}
                    x1_ld = {}
                    out_st = {}
                    dv_out = {}
                    for tq in range(NCH):
                        for o in range(2):
                            for k in range(32):
                                SYN.wait_ge(PE.sem,
                                            q2f_free.get(q2fi - 4, 0))
                                mq = sl_issue(SYN.dma_start(
                                    out=q2F[q2fi % 4][:, :],
                                    in_=q2_s[tq * 512:(tq + 1) * 512,
                                             k * 128:(k + 1) * 128],
                                    transpose=True), sQF, q2fi)
                                sl_wait(TEN, sQF, mq)
                                if k == 0:
                                    TEN.wait_ge(WST.sem, wd_rd)
                                    TEN.wait_ge(SC.sem, h_done_sc)
                                    TEN.wait_ge(DV.sem,
                                                iev.get((tq, o - 1),
                                                        iev.get((tq - 1, 1),
                                                                0)))
                                for t2 in range(4):
                                    TEN.matmul(
                                        mid_ps[t2][:, :],
                                        q2F[q2fi % 4][:, t2 * 128:
                                                      t2 * 128 + 128],
                                        wgb[:, k * 1024 + o * 512:
                                            k * 1024 + o * 512 + 512],
                                        start=(k == 0), stop=(k == 31))
                                q2f_free[q2fi] = pe_mark()
                                q2fi += 1
                            pe_dn = PE.n
                            for t2 in range(4):
                                tg = tq * 4 + t2
                                if o == 0:
                                    SYN.wait_ge(DV.sem, dv_out.get(tg - 4, 0))
                                    x1_ld[tg] = sl_issue(SYN.dma_start(
                                        out=ix1[t2][:, :],
                                        in_=x1_s[tg * TG:(tg + 1) * TG, :]),
                                        sXI, t2)
                                VEC.wait_ge(PE.sem, pe_dn)
                                sl_wait(VEC, sXI, x1_ld[tg])
                                if o == 0:
                                    sl_wait(VEC, sOUTS, out_st.get(tg - 4))
                                VEC.scalar_tensor_tensor(
                                    outbs[t2][:, o * 512:o * 512 + 512],
                                    mid_ps[t2][:, :], beta[:, tg:tg + 1],
                                    ix1[t2][:, o * 512:o * 512 + 512],
                                    ALU.mult, ALU.add)
                                dv_o = dv_mark()
                                if o == 1:
                                    dv_out[tg] = dv_o
                                    SYN.wait_ge(DV.sem, dv_o)
                                    out_st[tg] = sl_issue(SYN.dma_start(
                                        out=out[tg * TG:(tg + 1) * TG, :],
                                        in_=outbs[t2][:, :]), sOUTS, tg)
                            iev[(tq, o)] = DV.n
                    sl_wait_all(SYN, sOUTS)

    return nc


def kernel(L0_attn_x, L0_attn_W_in, L0_attn_A_log, L0_attn_W_B, L0_attn_W_C,
           L0_attn_W_dt, L0_attn_b_dt, L0_attn_D, L0_attn_output_gate,
           L0_attn_W_out, L0_ffn_w_gate, L0_ffn_w_up, L0_ffn_w_down):
    xs = np.ascontiguousarray(np.asarray(L0_attn_x, dtype=np.float32))
    weights = {
        "w_in": np.ascontiguousarray(np.asarray(L0_attn_W_in, np.float32)),
        "a_log": np.ascontiguousarray(np.asarray(L0_attn_A_log, np.float32)),
        "w_b": np.ascontiguousarray(np.asarray(L0_attn_W_B, np.float32)),
        "w_c": np.ascontiguousarray(np.asarray(L0_attn_W_C, np.float32)),
        "w_dt": np.ascontiguousarray(np.asarray(L0_attn_W_dt, np.float32)),
        "b_dt": np.ascontiguousarray(np.asarray(L0_attn_b_dt, np.float32)),
        "d_vec": np.ascontiguousarray(np.asarray(L0_attn_D, np.float32)),
        "gate": np.ascontiguousarray(np.asarray(L0_attn_output_gate, np.float32)),
        "w_out": np.ascontiguousarray(np.asarray(L0_attn_W_out, np.float32)),
        "w_g": np.ascontiguousarray(np.asarray(L0_ffn_w_gate, np.float32)),
        "w_u": np.ascontiguousarray(np.asarray(L0_ffn_w_up, np.float32)),
        "w_d": np.ascontiguousarray(np.asarray(L0_ffn_w_down, np.float32)),
    }
    nc = build_program()
    in_maps = []
    for b in range(8):
        m = {"x": np.ascontiguousarray(xs[b])}
        m.update(weights)
        in_maps.append(m)
    res = run_bass_kernel_spmd(nc, in_maps, core_ids=list(range(8)))
    return np.stack([np.asarray(r["out"], dtype=np.float32)
                     for r in res.results], axis=0)
